# version 26
# baseline (speedup 1.0000x reference)
import math
import numpy as np
import ml_dtypes
from contextlib import ExitStack

import concourse.bass as bass
import concourse.mybir as mybir
from concourse.bass_utils import run_bass_kernel_spmd

F32 = mybir.dt.float32
BF16 = mybir.dt.bfloat16
I32 = mybir.dt.int32
AL = mybir.AluOpType
AF = mybir.ActivationFunctionType
AX = mybir.AxisListType


def _bucket_scalar(d, nbuck=32, maxd=128):
    nb = nbuck // 2
    b = nb if d > 0 else 0
    ad = abs(d)
    me = nb // 2
    if ad < me:
        return b + ad
    large = me + int(math.log(max(ad, 1) / me) / math.log(maxd / me) * (nb - me))
    return b + min(large, nb - 1)


def _bucket_np(d, nbuck=32, maxd=128):
    d = np.asarray(d, dtype=np.int64)
    nb = nbuck // 2
    buckets = (d > 0).astype(np.int64) * nb
    ad = np.abs(d)
    me = nb // 2
    small = ad < me
    large = me + (np.log(np.maximum(ad, 1).astype(np.float32) / me)
                  / math.log(maxd / me) * (nb - me)).astype(np.int64)
    large = np.minimum(large, nb - 1)
    return buckets + np.where(small, ad, large)


class Cfg:
    def __init__(self, D=1024, C=4096, H=16, DK=64, S=4096, B=4,
                 G1tiles=3, G2tiles=3, NBUCK=32, MAXD=128):
        self.D, self.C, self.H, self.DK, self.S, self.B = D, C, H, DK, S, B
        self.R1 = S + 4
        self.R2 = S // 2 + 4
        self.ND = D // 128
        self.NC = C // 128
        self.NKT = S // 128
        self.NP = H // 2
        self.G1 = G1tiles * 128
        self.G2 = G2tiles * 128
        self.NBUCK, self.MAXD = NBUCK, MAXD
        nb = NBUCK // 2
        bmax = nb // 2
        while bmax < S and _bucket_scalar(bmax, NBUCK, MAXD) < NBUCK - 1:
            bmax += 1
        self.BMAX = bmax
        self.BW = 2 * self.BMAX + 126
        self.QG = []
        q = 0
        while q < self.R2:
            w = min(1024, self.R2 - q)
            self.QG.append((q, w))
            q += w

    def tiles(self, R):
        out, r, i = [], 0, 0
        while r < R:
            rows = min(128, R - r)
            out.append((i, r, rows))
            i += 1
            r += rows
        return out


class Op:
    __slots__ = ("idx", "eng", "fn", "deps", "dma_sem", "val")
    def __init__(self, idx, eng, fn, deps, dma_sem):
        self.idx, self.eng, self.fn, self.deps, self.dma_sem = \
            idx, eng, fn, deps, dma_sem
        self.val = 0


BATCH_SEMS = {"dC", "dBd"}


class Sched:
    def __init__(self):
        self.ops = []
        self.res = {}

    def op(self, eng, fn, reads=(), writes=(), deps=(), dma_sem=None):
        dd = set(d for d in deps if d is not None)
        for r in reads:
            w, _ = self.res.get(r, (None, []))
            if w is not None:
                dd.add(w)
        for wkey in writes:
            w, rs = self.res.get(wkey, (None, []))
            if w is not None:
                dd.add(w)
            dd.update(rs)
        idx = len(self.ops)
        self.ops.append(Op(idx, eng, fn, dd, dma_sem))
        for r in reads:
            w, rs = self.res.setdefault(r, (None, []))
            rs.append(idx)
        for wkey in writes:
            self.res[wkey] = (idx, [])
        return idx

    def dma(self, sem, out_ap, in_ap, reads=(), writes=(), deps=(), eng="Y",
            transpose=False):
        def fn(e, out_ap=out_ap, in_ap=in_ap, transpose=transpose):
            return e.dma_start(out=out_ap, in_=in_ap, transpose=transpose)
        return self.op(eng, fn, reads, writes, deps, dma_sem=sem)

    def barrier(self):
        last = {}
        for o in self.ops:
            if o.fn is None and o.dma_sem is None:
                continue
            last[o.eng if o.dma_sem is None else o.dma_sem] = o.idx
        deps = list(last.values())
        for eng in ("T", "V", "S", "G", "Y"):
            self.op(eng, None, deps=deps)

    def emit(self, nc, es):
        need = set()
        for o in self.ops:
            for d in o.deps:
                a = self.ops[d]
                if a.dma_sem is not None:
                    continue
                if a.eng != o.eng:
                    need.add(d)
                elif a.eng != "T":
                    need.add(d)
        sems = {}
        cnt = {}
        def sem_of(o):
            name = o.dma_sem if o.dma_sem is not None else ("e" + o.eng)
            if name not in sems:
                sems[name] = es.enter_context(nc.semaphore(name))
                cnt[name] = 0
            return name
        for o in self.ops:
            if o.dma_sem is not None:
                name = sem_of(o)
                cnt[name] += 16
                o.val = cnt[name]
            elif o.idx in need:
                name = sem_of(o)
                cnt[name] += 1
                o.val = cnt[name]
        for o in self.ops:
            if o.dma_sem in BATCH_SEMS:
                o.val = cnt[o.dma_sem]
        streams = {e: [] for e in ("T", "V", "S", "G", "Y")}
        for o in self.ops:
            streams[o.eng].append(o)
        sched = self

        def run(engobj, eng):
            waited = {}
            for o in streams[eng]:
                want = {}
                for d in o.deps:
                    a = sched.ops[d]
                    if a.eng == eng and a.dma_sem is None and eng == "T":
                        continue
                    if a.val == 0:
                        continue
                    nm = a.dma_sem if a.dma_sem is not None else ("e" + a.eng)
                    if want.get(nm, 0) < a.val:
                        want[nm] = a.val
                for nm, v in want.items():
                    if waited.get(nm, 0) >= v:
                        continue
                    waited[nm] = v
                    engobj.wait_ge(sems[nm], v)
                if o.fn is None:
                    continue
                inst = o.fn(engobj)
                if o.val and o.dma_sem is None:
                    inst.then_inc(sems["e" + o.eng], 1)
                elif o.dma_sem is not None:
                    inst.then_inc(sems[o.dma_sem], 16)

        block = es.enter_context(nc.Block())

        @block.tensor
        def _(t):
            run(t, "T")

        @block.vector
        def _(v):
            run(v, "V")

        @block.scalar
        def _(s):
            run(s, "S")

        @block.gpsimd
        def _(g):
            run(g, "G")

        @block.sync
        def _(y):
            run(y, "Y")


class Build:
    def __init__(self, cfg):
        self.cfg = cfg
        self.nc = bass.Bass("TRN2", target_bir_lowering=False, debug=False)
        self.es = ExitStack()
        self.sc = Sched()
        self._nsb = 0

    def sb(self, name, shape, dt):
        return self.es.enter_context(self.nc.sbuf_tensor(name, shape, dt))

    def build(self):
        cfg, nc = self.cfg, self.nc
        D, C, R1, R2 = cfg.D, cfg.C, cfg.R1, cfg.R2

        self.x0d = nc.dram_tensor("x0", [R1, D], F32, kind="ExternalInput").ap()
        self.w1d = nc.dram_tensor("w1", [C, D], F32, kind="ExternalInput").ap()
        self.w2d = nc.dram_tensor("w2", [D, C], F32, kind="ExternalInput").ap()
        self.cwd = nc.dram_tensor("cw", [128, cfg.NC * 3], F32, kind="ExternalInput").ap()
        self.cbd = nc.dram_tensor("cb", [128, cfg.NC], F32, kind="ExternalInput").ap()
        self.wqd = nc.dram_tensor("wq", [D, D], F32, kind="ExternalInput").ap()
        self.wkd = nc.dram_tensor("wk", [D, D], F32, kind="ExternalInput").ap()
        self.wvd = nc.dram_tensor("wv", [D, D], F32, kind="ExternalInput").ap()
        self.wod = nc.dram_tensor("wo", [D, D], F32, kind="ExternalInput").ap()
        self.bandd = nc.dram_tensor("band", [cfg.H, 128, cfg.BW], F32,
                                    kind="ExternalInput").ap()
        self.ccd = nc.dram_tensor("cc", [128, 2 * cfg.H], F32, kind="ExternalInput").ap()
        self.identd = nc.dram_tensor("ident", [128, 128], F32, kind="ExternalInput").ap()
        self.identbd = nc.dram_tensor("identb", [128, 128], BF16, kind="ExternalInput").ap()
        self.onesd = nc.dram_tensor("ones", [128, 128], F32, kind="ExternalInput").ap()
        self.outd = nc.dram_tensor("out", [R2, D], F32, kind="ExternalOutput").ap()

        self.w1ts = nc.dram_tensor("w1ts", [D, C], BF16).ap()
        self.t2s = nc.dram_tensor("t2s", [C, D], BF16).ap()
        self.x1s = nc.dram_tensor("x1s", [R1, D], F32).ap()
        self.kscr = nc.dram_tensor("kscr", [D, R1], BF16).ap()
        self.vscr = nc.dram_tensor("vscr", [R1, D], BF16).ap()
        self.aots = nc.dram_tensor("aots", [D, R2], BF16).ap()
        self.x2s = nc.dram_tensor("x2s", [R2, D], F32).ap()

        self.ident = self.sb("identsb", [128, 128], F32)
        self.identb = self.sb("identbsb", [128, 128], BF16)
        self.onessb = self.sb("onessb", [128, 128], F32)
        self.cw = self.sb("cwsb", [128, cfg.NC * 3], F32)
        self.cb = self.sb("cbsb", [128, cfg.NC], F32)
        self.ccsb = self.sb("ccsb", [128, 2 * cfg.H], F32)
        self.m1fold = self.sb("m1fold", [128, 1], F32)
        self.m2fold = self.sb("m2fold", [128, 1], F32)
        self.r1rec = self.sb("r1rec", [128, 1], F32)
        self.r2rec = self.sb("r2rec", [128, 1], F32)
        self.stmp = self.sb("stmp", [128, 8], F32)

        self.psA = self.es.enter_context(nc.psum_tensor([128, 1024], F32))
        self.psB = self.es.enter_context(nc.psum_tensor([128, 1024], F32))
        self.psC = self.es.enter_context(nc.psum_tensor([128, 1024], F32))
        self.ps1 = self.es.enter_context(nc.psum_tensor([128, 512], F32))
        self.psX = self.es.enter_context(nc.psum_tensor([128, 512], F32))

        tok0 = self.p0_consts()
        with ExitStack() as ph:
            self._ph = ph
            self.p0_weights(tok0)
        self.sc.barrier()
        with ExitStack() as ph:
            self._ph = ph
            self.p1_ffn1()
        self.sc.barrier()
        with ExitStack() as qph:
            self.qt = qph.enter_context(
                self.nc.sbuf_tensor("qtsb", [128, cfg.ND * R2], BF16))
            with ExitStack() as ph:
                self._ph = ph
                self.p2_proj()
            self.sc.barrier()
            with ExitStack() as ph:
                self._ph = ph
                self.p3_attn()
        self.sc.barrier()
        with ExitStack() as ph:
            self._ph = ph
            self.p4_wo()
        self.sc.barrier()
        with ExitStack() as ph:
            self._ph = ph
            self.p5_ffn2()
        self.p6_out()

        self.sc.emit(nc, self.es)
        self.es.close()
        return nc

    def psb(self, name, shape, dt):
        self._nsb += 1
        return self._ph.enter_context(
            self.nc.sbuf_tensor(f"{name}_{self._nsb}", shape, dt))

    def p0_consts(self):
        sc = self.sc
        sc.dma("dC", self.ident[:], self.identd[:, :], writes=["ident"])
        sc.dma("dC", self.identb[:], self.identbd[:, :], writes=["identb"])
        sc.dma("dC", self.onessb[:], self.onesd[:, :], writes=["ones"])
        sc.dma("dC", self.cw[:], self.cwd[:, :], writes=["cw"])
        sc.dma("dC", self.cb[:], self.cbd[:, :], writes=["cb"])
        return sc.dma("dC", self.ccsb[:], self.ccd[:, :], writes=["cc"])

    def p0_weights(self, tok0):
        cfg, sc = self.cfg, self.sc
        D, C = cfg.D, cfg.C

        trc = {"n": 0}

        def prep(widx, wd, n_tiles, tcols, outs, m_elems, foldmul, mfold, mrec,
                 nring):
            tag = f"w{widx}"
            wring = [self.psb(f"{tag}ring{i}", [128, tcols], F32)
                     for i in range(nring)]
            tern = [self.psb(f"{tag}tern{i}", [128, tcols], F32)
                    for i in range(2)]
            tlo = self.psb(f"{tag}tlo", [128, tcols], F32)
            tstage = [self.psb(f"{tag}stage{i}", [128, tcols], BF16)
                      for i in range(2)]
            acc = f"{tag}acc"
            accs = self.psb(f"{tag}accs", [128, 4], F32)
            nj = tcols // 128
            last_wb = {}
            sc.op("V", lambda e: e.memset(accs[:, 2:3], 0.0), writes=[acc])
            ldq = "Y" if widx == 0 else "S"
            for i in range(n_tiles):
                sc.dma(f"d{tag}p{i % nring}", wring[i % nring][:, :],
                       wd[128 * i:128 * (i + 1), :],
                       writes=[(tag, i % nring)], deps=[tok0], eng=ldq)
                sc.op("V", lambda e, i=i: e.reduce_sum(
                    accs[:, 0:1], wring[i % nring][:, :], axis=AX.X,
                    apply_absolute_value=True),
                    reads=[(tag, i % nring)], writes=[acc + "r"])
                sc.op("V", lambda e: e.tensor_tensor(
                    accs[:, 2:3], accs[:, 2:3], accs[:, 0:1], AL.add),
                    reads=[acc + "r", acc], writes=[acc])
            sc.op("T", lambda e: e.matmul(
                self.ps1[:, widx:widx + 1], self.onessb[:, :], accs[:, 2:3],
                start=True, stop=True), reads=[acc, "ones"],
                writes=[("ps1m", widx)])
            sc.op("V", lambda e: e.tensor_scalar(
                accs[:, 1:2], self.ps1[:, widx:widx + 1], 1.0 / m_elems, 1e-5,
                AL.mult, AL.max), reads=[("ps1m", widx)], writes=[acc + "m"])
            sc.op("V", lambda e: e.tensor_scalar(
                mfold[:], accs[:, 1:2], foldmul, None, AL.mult),
                reads=[acc + "m"], writes=[("mfold", widx)])
            rc = sc.op("V", lambda e: e.reciprocal(mrec[:], accs[:, 1:2]),
                       reads=[acc + "m"], writes=[("mrec", widx)])
            for i in range(n_tiles):
                sc.dma(f"d{tag}p{i % nring}", wring[i % nring][:, :],
                       wd[128 * i:128 * (i + 1), :],
                       writes=[(tag, i % nring)], deps=[rc], eng=ldq)
                sp = i % 2
                sc.op("V", lambda e, i=i, sp=sp: e.tensor_scalar(
                    tern[sp][:, :], wring[i % nring][:, :], mrec[:], 0.5,
                    AL.mult, AL.is_gt),
                    reads=[(tag, i % nring), ("mrec", widx)],
                    writes=[(tag + "t", sp)])
                sc.op("V", lambda e, i=i: e.tensor_scalar(
                    tlo[:, :], wring[i % nring][:, :], mrec[:], -0.5,
                    AL.mult, AL.is_lt),
                    reads=[(tag, i % nring), ("mrec", widx)],
                    writes=[(tag + "lo",)])
                sc.op("V", lambda e, sp=sp: e.tensor_tensor(
                    tern[sp][:, :], tern[sp][:, :], tlo[:, :], AL.subtract),
                    reads=[(tag + "t", sp), (tag + "lo",)],
                    writes=[(tag + "t", sp)])
                for j in range(nj):
                    pp = trc["n"] % 2
                    trc["n"] += 1
                    pst = self.psA if pp == 0 else self.psB
                    sc.op("T", lambda e, j=j, sp=sp, pst=pst: e.transpose(
                        pst[:, 0:128], tern[sp][:, 128 * j:128 * (j + 1)],
                        self.ident[:, :]),
                        reads=[(tag + "t", sp), "ident"],
                        writes=[("p0ps", pp)])
                    sc.op("S", lambda e, j=j, sp=sp, pst=pst: e.copy(
                        tstage[sp][:, 128 * j:128 * (j + 1)], pst[:, 0:128]),
                        reads=[("p0ps", pp)], writes=[(tag + "s", i, j)],
                        deps=[last_wb.get(sp)])
                last_wb[sp] = sc.dma(
                    f"d{tag}o",
                    outs.rearrange("(j p) c -> p j c", p=128)
                    [:, :, 128 * i:128 * (i + 1)],
                    tstage[sp].ap().rearrange("p (j c) -> p j c", c=128),
                    reads=[(tag + "s", i, j) for j in range(nj)],
                    writes=[("wscr", widx)])

        prep(0, self.w1d, C // 128, D, self.w1ts, float(C) * D, 1.0 / 127.0,
             self.m1fold, self.r1rec, 3)
        prep(1, self.w2d, D // 128, C, self.t2s, float(C) * D, 0.5 / 127.0,
             self.m2fold, self.r2rec, 2)

    def ln_tile(self, pb, xkey, xap, rows, out_ap, out_key, quant, skey=None,
                esc=None, mfold_key=None):
        cfg, sc = self.cfg, self.sc
        b = pb
        r = rows
        D_ = xap.shape[1]
        nh = (D_ + 511) // 512
        for j in range(nh):
            w_ = min(512, D_ - 512 * j)
            sc.op("V", lambda e, j=j, w_=w_: e.bn_stats(
                b["bn6"][:r, 6 * j:6 * j + 6], xap[:, 512 * j:512 * j + w_]),
                reads=[xkey], writes=["bn6"])
        sc.op("V", lambda e, nh=nh: e.bn_aggr(b["mv"][:r], b["bn6"][:r, :6 * nh]),
              reads=["bn6"], writes=["mv"])
        sc.op("V", lambda e: e.tensor_scalar(
            b["mv"][:r, 1:2], b["mv"][:r, 1:2], 1e-5, None, AL.add),
            reads=["mv"], writes=["mv"])
        sc.op("S", lambda e: e.sqrt(b["sq"][:r], b["mv"][:r, 1:2]),
              reads=["mv"], writes=["sq"])
        sc.op("V", lambda e: e.reciprocal(b["r0"][:r], b["sq"][:r]),
              reads=["sq"], writes=["r0"])
        sc.op("V", lambda e: e.tensor_tensor(
            b["r2"][:r], b["r0"][:r], b["r0"][:r], AL.mult),
            reads=["r0"], writes=["r2"])
        sc.op("V", lambda e: e.tensor_tensor(
            b["r2"][:r], b["r2"][:r], b["mv"][:r, 1:2], AL.mult),
            reads=["r2", "mv"], writes=["r2"])
        sc.op("V", lambda e: e.tensor_scalar(
            b["r2"][:r], b["r2"][:r], -0.5, 1.5, AL.mult, AL.add),
            reads=["r2"], writes=["r2"])
        sc.op("V", lambda e: e.tensor_tensor(
            b["r0"][:r], b["r0"][:r], b["r2"][:r], AL.mult),
            reads=["r0", "r2"], writes=["r0"])
        if not quant:
            t = sc.op("V", lambda e: e.tensor_scalar(
                out_ap, xap, b["mv"][:r, 0:1], b["r0"][:r],
                AL.subtract, AL.mult), reads=[xkey, "mv", "r0"], writes=[out_key])
            return t
        sc.op("V", lambda e: e.tensor_scalar(
            b["u"][:r], xap, b["mv"][:r, 0:1], b["r0"][:r],
            AL.subtract, AL.mult), reads=[xkey, "mv", "r0"], writes=["u"])
        sc.op("V", lambda e: e.reduce_max(
            b["s4"][:r, 0:1], b["u"][:r], axis=AX.X, apply_absolute_value=True),
            reads=["u"], writes=["s4"])
        sc.op("V", lambda e: e.tensor_scalar(
            b["s4"][:r, 1:2], b["s4"][:r, 0:1], 1e-5, None, AL.max),
            reads=["s4"], writes=["s4"])
        sc.op("V", lambda e: e.reciprocal(b["s4"][:r, 2:3], b["s4"][:r, 1:2]),
              reads=["s4"], writes=["s4"])
        sc.op("V", lambda e: e.tensor_scalar(
            b["s4"][:r, 3:4], b["s4"][:r, 2:3], 127.0, None, AL.mult),
            reads=["s4"], writes=["s4"])
        if r < 128:
            sc.op("V", lambda e: e.memset(esc[0][:, esc[1]:esc[1] + 1], 0.0),
                  writes=[esc[2]])
        et = sc.op("V", lambda e: e.tensor_scalar(
            esc[0][:r, esc[1]:esc[1] + 1], b["s4"][:r, 1:2],
            self.m1fold[:r] if mfold_key == "m1" else self.m2fold[:r], None,
            AL.mult), reads=["s4", ("mfold", 0 if mfold_key == "m1" else 1)],
            writes=[esc[2]])
        q = self.round_chain(b, "u", b["u"], b["s4"][:, 3:4], r, cfg.D,
                             out_ap, out_key)
        return q, et

    def round_chain(self, b, vkey, v_ap, s_ap, rows, width, out_ap, out_key,
                    s_is_tile=False, skey=None, qoff=0, bpar=""):
        sc = self.sc
        r, w = rows, width
        q0_, q1_ = qoff, qoff + w
        vkeys = vkey if isinstance(vkey, list) else [vkey]
        sreads = vkeys + ([skey] if skey else ["s4"])
        if s_is_tile:
            sc.op("V", lambda e: e.tensor_tensor(
                b["qa"][:r, q0_:q1_], v_ap, s_ap, AL.mult), reads=sreads,
                writes=["qa" + bpar])
            sc.op("V", lambda e: e.tensor_scalar(
                b["qi"][:r, q0_:q1_], b["qa"][:r, q0_:q1_], 256.75, None,
                AL.add), reads=["qa" + bpar], writes=["qi" + bpar])
            sc.op("V", lambda e: e.scalar_tensor_tensor(
                b["qc"][:r, q0_:q1_], b["qi"][:r, q0_:q1_], 256.5,
                b["qa"][:r, q0_:q1_], AL.subtract, AL.is_gt),
                reads=["qi" + bpar, "qa" + bpar], writes=["qc" + bpar])
        else:
            v_ap = v_ap[:r, :w]
            s_ap = s_ap[:r, :]
            sc.op("V", lambda e: e.tensor_scalar(
                b["qa"][:r, q0_:q1_], v_ap, s_ap, 256.5, AL.mult, AL.add),
                reads=sreads, writes=["qa" + bpar])
            sc.op("V", lambda e: e.tensor_scalar(
                b["qi"][:r, q0_:q1_], b["qa"][:r, q0_:q1_], 0.25, None,
                AL.add), reads=["qa" + bpar], writes=["qi" + bpar])
            sc.op("V", lambda e: e.tensor_tensor(
                b["qc"][:r, q0_:q1_], b["qi"][:r, q0_:q1_], b["qa"][:r, q0_:q1_],
                AL.is_gt), reads=["qi" + bpar, "qa" + bpar],
                writes=["qc" + bpar])
        return sc.op("V", lambda e: e.scalar_tensor_tensor(
            out_ap, b["qi"][:r, q0_:q1_], 256.0, b["qc"][:r, q0_:q1_],
            AL.subtract, AL.subtract), reads=["qi" + bpar, "qc" + bpar],
            writes=[out_key])

    def ffn(self, tag, R, G, x_src, x_out):
        cfg, sc = self.cfg, self.sc
        D, C, ND, NC = cfg.D, cfg.C, cfg.ND, cfg.NC
        Gt = G // 128
        tiles = cfg.tiles(R)
        ntiles = len(tiles)
        ngroups = (ntiles + Gt - 1) // Gt

        b = {
            "bn6": self.psb(tag + "bn6", [128, 12], F32),
            "mv": self.psb(tag + "mv", [128, 2], F32),
            "sq": self.psb(tag + "sq", [128, 1], F32),
            "r0": self.psb(tag + "r0", [128, 1], F32),
            "r2": self.psb(tag + "r2", [128, 1], F32),
            "u": self.psb(tag + "u", [128, D], F32),
            "s4": self.psb(tag + "s4", [128, 4], F32),
            "qa": self.psb(tag + "qa", [128, max(D, 4 * G)], F32),
            "qc": self.psb(tag + "qc", [128, max(D, 4 * G)], F32),
            "qi": self.psb(tag + "qi", [128, max(D, 4 * G)], I32),
        }
        b2 = b
        uqbuf = [self.psb(tag + f"uqb{i}", [128, D], BF16) for i in range(2)]
        W_ = 128 * (Gt + 2)
        gbuf = [self.psb(tag + f"gb{i}", [128, ND * W_], BF16) for i in range(3)]
        escr = self.psb(tag + "escr", [128, 3 * Gt + 3], F32)
        w1sb = [self.psb(tag + f"w1s{i}", [128, ND * 128], BF16) for i in range(3)]
        t2sb = [self.psb(tag + f"t2s{i}", [128, D], BF16) for i in range(3)]
        Hsb = [self.psb(tag + f"H{i}", [128, G + 2], F32) for i in range(2)]
        cv1 = [self.psb(tag + f"cv1{i}", [128, G], F32) for i in range(2)]
        cv2 = [self.psb(tag + f"cv2{i}", [128, G], F32) for i in range(2)]
        ysb = self.psb(tag + "y", [128, NC * G], F32)
        yab = [self.psb(tag + f"ya{i}", [128, G], F32) for i in range(2)]
        Msb = [self.psb(tag + f"M{i}", [128, G], F32) for i in range(2)]
        s2bc = self.psb(tag + "s2bc", [128, G], F32)
        e1bc = self.psb(tag + "e1bc", [128, G + 2], F32)
        a2c = self.psb(tag + "a2c", [128, Gt], F32)
        e2c = self.psb(tag + "e2c", [128, Gt], F32)
        s2c = self.psb(tag + "s2c", [128, 3], F32)
        yq = [self.psb(tag + "yq0", [128, NC * G], BF16)]

        def esc_slot(i):
            return i % (3 * Gt + 3)

        quant_done = {}

        def do_quant(i):
            if i in quant_done or i >= ntiles:
                return
            (_, r0, rows) = tiles[i]
            g = i // Gt
            par = g % 3
            k = i - g * Gt
            xap, xkey, xdeps = x_src(i, r0, rows)
            uq = uqbuf[i % 2]
            uqk = ("uqbuf", i % 2)
            if rows < 128:
                sc.op("V", lambda e, uq=uq: e.memset(uq[:, :], 0.0),
                      writes=[uqk])
            qt, et = self.ln_tile(b, xkey, xap, rows, uq[:rows, :], uqk,
                                  True, esc=(escr, esc_slot(i), ("esc", esc_slot(i))),
                                  mfold_key="m1")
            for fc in range(ND):
                targets = [(par, (k + 1) * 128)]
                if k == Gt - 1 and (g + 1) * Gt < ntiles:
                    targets.append(((g + 1) % 3, 0))
                if k == 0 and g >= 1:
                    targets.append(((g - 1) % 3, (Gt + 1) * 128))
                for (ti_, (tb, c0)) in enumerate(targets):
                    okey = ("tqo", fc) if ti_ == 0 else ("tqxo", fc)
                    sc.dma(f"dTq{fc}" if ti_ == 0 else f"dTqx{fc}",
                           gbuf[tb][:, fc * W_ + c0:fc * W_ + c0 + 128],
                           uq[:, 128 * fc:128 * (fc + 1)],
                           reads=[uqk], writes=[("gb", tb, fc), okey],
                           eng="S", transpose=True)
            quant_done[i] = True

        def edge_memsets(g):
            par = g % 3
            i0 = g * Gt
            gtl = [t for t in tiles[i0:i0 + Gt]]
            grl = sum(rr for (_, _, rr) in gtl)
            for fc in range(ND):
                base = fc * W_
                if g == 0:
                    sc.op("V", lambda e, base=base, par=par: e.memset(
                        gbuf[par][:, base + 127:base + 128], 0.0),
                        writes=[("gb", par, fc)])
                if i0 + Gt >= ntiles:
                    sc.op("V", lambda e, base=base, grl=grl, par=par: e.memset(
                        gbuf[par][:, base + 128 + grl:base + W_], 0.0),
                        writes=[("gb", par, fc)])

        edge_memsets(0)
        for (i, _, _) in tiles[0:Gt]:
            do_quant(i)
        if ngroups > 1:
            edge_memsets(1)
            for (i, _, _) in tiles[Gt:2 * Gt]:
                do_quant(i)
        pend_mm2 = [None]
        for g in range(ngroups):
            par = g % 3
            i0 = g * Gt
            gtiles = [t for t in tiles[i0:i0 + Gt]]
            gr = sum(rr for (_, _, rr) in gtiles)

            def bc(col0, w, slot, ekey):
                aa = col0
                while aa < col0 + w:
                    bb = min(col0 + w, ((aa // 512) + 1) * 512)
                    j0 = aa - col0
                    sc.op("T", lambda e, aa=aa, bb=bb, j0=j0, slot=slot: e.matmul(
                        self.ps1[:, aa:bb],
                        escr[:, slot:slot + 1].broadcast_to([128, 128]),
                        self.ident[:, j0:j0 + (bb - aa)], start=True, stop=True),
                        reads=[ekey, "ident"], writes=["ps1"])
                    aa = bb
            if i0 - 1 >= 0:
                bc(0, 1, esc_slot(i0 - 1), ("esc", esc_slot(i0 - 1)))
            for (i, _, rows) in gtiles:
                k = i - i0
                bc(1 + 128 * k, rows, esc_slot(i), ("esc", esc_slot(i)))
            if i0 + Gt < ntiles:
                bc(1 + gr, 1, esc_slot(i0 + Gt), ("esc", esc_slot(i0 + Gt)))
            c_lo = 0 if i0 - 1 >= 0 else 1
            c_hi = gr + 2 if i0 + Gt < ntiles else gr + 1
            cp = sc.op("V", lambda e, c_lo=c_lo, c_hi=c_hi: e.tensor_copy(
                e1bc[:, c_lo:c_hi], self.ps1[:, c_lo:c_hi]),
                reads=["ps1"], writes=["e1bc"])
            if i0 - 1 < 0:
                sc.op("V", lambda e: e.memset(e1bc[:, 0:1], 0.0), writes=["e1bc"])
            if i0 + Gt >= ntiles:
                sc.op("V", lambda e, gr=gr: e.memset(e1bc[:, 1 + gr:gr + 2], 0.0),
                      writes=["e1bc"])

            sc.op("V", lambda e, gr=gr: e.memset(Msb[0][:, :gr], 0.0),
                  writes=[("M", 0)])
            sc.op("V", lambda e, gr=gr: e.memset(Msb[1][:, :gr], 0.0),
                  writes=[("M", 1)])

            def st_mm1(ct):
                wp = ct % 3
                sc.dma(f"dW1{wp}",
                       w1sb[wp].ap().rearrange("p (f c) -> p f c", f=ND)[:, :, :],
                       self.w1ts.rearrange("(fo p) c -> p fo c", p=128)
                       [:, :, 128 * ct:128 * (ct + 1)],
                       reads=[("wscr", 0)], writes=[("w1sb", wp)])
                mmoff = 0 if ct % 2 == 0 else 512
                mmps = self.psC[:, mmoff:mmoff + 512]
                mkey = "psC0" if ct % 2 == 0 else "psC1"
                for fc in range(ND):
                    base = fc * W_ + 127
                    aa = 0
                    while aa < gr + 2:
                        bb = min(gr + 2, aa + 512)
                        sc.op("T", lambda e, fc=fc, aa=aa, bb=bb, base=base,
                              mmps=mmps, wp=wp, par=par: e.matmul(
                            mmps[:, aa:bb],
                            w1sb[wp].ap().rearrange("p (f c) -> p f c", f=ND)[:, fc, :],
                            gbuf[par][:, base + aa:base + bb],
                            start=(fc == 0), stop=(fc == ND - 1)),
                            reads=[("w1sb", wp), ("gb", par, fc)],
                            writes=[mkey])
                        aa = bb

            def st_evac(ct):
                hp = ct % 2
                mmoff = 0 if ct % 2 == 0 else 512
                mmps = self.psC[:, mmoff:mmoff + 512]
                mkey = "psC0" if ct % 2 == 0 else "psC1"
                sc.op("V", lambda e, mmps=mmps, gr=gr, hp=hp: e.tensor_tensor(
                    Hsb[hp][:, 0:gr + 2], mmps[:, 0:gr + 2], e1bc[:, 0:gr + 2],
                    AL.mult), reads=[mkey, "e1bc"], writes=[("H", hp)])

            def st_tap1(ct):
                hp = ct % 2
                sc.op("S", lambda e, ct=ct, gr=gr, hp=hp: e.activation(
                    cv1[hp][:, 0:gr], Hsb[hp][:, 1:gr + 1], AF.Identity,
                    bias=self.cb[:, ct:ct + 1],
                    scale=self.cw[:, 3 * ct + 1:3 * ct + 2]),
                    reads=[("H", hp), "cw", "cb"], writes=[("cv1", hp)])

            def st_taps(ct):
                hp = ct % 2
                sc.op("V", lambda e, ct=ct, gr=gr, hp=hp: e.scalar_tensor_tensor(
                    cv2[hp][:, 0:gr], Hsb[hp][:, 0:gr],
                    self.cw[:, 3 * ct:3 * ct + 1], cv1[hp][:, 0:gr],
                    AL.mult, AL.add), reads=[("H", hp), "cw", ("cv1", hp)],
                    writes=[("cv2", hp)])
                sc.op("V", lambda e, ct=ct, gr=gr, hp=hp: e.scalar_tensor_tensor(
                    cv2[hp][:, 0:gr], Hsb[hp][:, 2:gr + 2],
                    self.cw[:, 3 * ct + 2:3 * ct + 3], cv2[hp][:, 0:gr],
                    AL.mult, AL.add), reads=[("H", hp), "cw", ("cv2", hp)],
                    writes=[("cv2", hp)])

            def st_silu(ct):
                hp = ct % 2
                sc.op("S", lambda e, ct=ct, gr=gr, hp=hp: e.activation(
                    ysb[:, G * ct:G * ct + gr], cv2[hp][:, 0:gr], AF.Silu),
                    reads=[("cv2", hp)], writes=[("y", ct)])
                sc.op("S", lambda e, ct=ct, gr=gr, hp=hp: e.activation(
                    yab[hp][:, 0:gr], ysb[:, G * ct:G * ct + gr], AF.Abs),
                    reads=[("y", ct)], writes=[("yab", hp)])

            def st_max(ct):
                hp = ct % 2
                sc.op("V", lambda e, gr=gr, hp=hp: e.tensor_tensor(
                    Msb[hp][:, 0:gr], Msb[hp][:, 0:gr], yab[hp][:, 0:gr],
                    AL.max), reads=[("yab", hp), ("M", hp)], writes=[("M", hp)])

            stages = [st_mm1, st_evac, st_tap1, st_taps, st_silu, st_max]
            for t_ in range(NC + len(stages) - 1):
                for si in reversed(range(len(stages))):
                    ct = t_ - si
                    if 0 <= ct < NC:
                        stages[si](ct)

            if pend_mm2[0] is not None:
                pend_mm2[0]()
                pend_mm2[0] = None
            if g + 2 < ngroups:
                edge_memsets(g + 2)
                for (i, _, _) in tiles[(g + 2) * Gt:(g + 3) * Gt]:
                    do_quant(i)

            sc.op("V", lambda e, gr=gr: e.tensor_tensor(
                Msb[0][:, 0:gr], Msb[0][:, 0:gr], Msb[1][:, 0:gr], AL.max),
                reads=[("M", 0), ("M", 1)], writes=[("M", 0)])
            for (i, _, rows) in gtiles:
                k = i - i0
                sc.op("T", lambda e, k=k, rows=rows: e.transpose(
                    self.ps1[0:rows, 0:128],
                    Msb[0][:, 128 * k:128 * k + rows], self.ident[:, :]),
                    reads=[("M", 0), "ident"], writes=["ps1"])
                sc.op("V", lambda e, k=k, rows=rows: e.reduce_max(
                    a2c[:rows, k:k + 1], self.ps1[:rows, 0:128], axis=AX.X),
                    reads=["ps1"], writes=[("a2c", k)])
                sc.op("V", lambda e, k=k, rows=rows: e.tensor_scalar(
                    a2c[:rows, k:k + 1], a2c[:rows, k:k + 1], 1e-5, None, AL.max),
                    reads=[("a2c", k)], writes=[("a2c", k)])
                sc.op("V", lambda e, k=k, rows=rows: e.tensor_scalar(
                    e2c[:rows, k:k + 1], a2c[:rows, k:k + 1], self.m2fold[:rows],
                    None, AL.mult), reads=[("a2c", k), ("mfold", 1)],
                    writes=[("e2c", k)])
                sc.op("V", lambda e, k=k, rows=rows: e.reciprocal(
                    s2c[:rows, 0:1], a2c[:rows, k:k + 1]),
                    reads=[("a2c", k)], writes=["s2c"])
                if rows < 128:
                    sc.op("V", lambda e: e.memset(s2c[:, 1:2], 0.0),
                          writes=["s2c"])
                sc.op("V", lambda e, k=k, rows=rows: e.tensor_scalar(
                    s2c[:rows, 1:2], s2c[:rows, 0:1], 127.0, None, AL.mult),
                    reads=["s2c"], writes=["s2c"])
                sc.op("T", lambda e, k=k, rows=rows: e.matmul(
                    self.ps1[:, 128 + 128 * k:128 + 128 * k + rows],
                    s2c[:, 1:2].broadcast_to([128, 128]),
                    self.ident[:, 0:rows], start=True, stop=True),
                    reads=["s2c", "ident"], writes=["ps1"])
            sc.op("V", lambda e, gr=gr: e.tensor_copy(
                s2bc[:, 0:gr], self.ps1[:, 128:128 + gr]),
                reads=["ps1"], writes=["s2bc"])

            ypar = 0
            ysb3 = ysb.ap().rearrange("p (c g) -> p c g", g=G)
            yq3 = yq[ypar].ap().rearrange("p (c g) -> p c g", g=G)
            for ct0 in range(0, NC, 2):
                bs = min(2, NC - ct0)
                bp = str((ct0 // 2) % 2)
                v_ap = ysb3[:, ct0:ct0 + bs, 0:gr]
                s_ap = s2bc[:, 0:gr].unsqueeze(1).broadcast_to([128, bs, gr])
                o_ap = yq3[:, ct0:ct0 + bs, 0:gr]
                ykeys = [("y", ct0 + j) for j in range(bs)]
                q = self.round_chain(
                    b2, ykeys, v_ap, s_ap, 128, bs * gr, o_ap,
                    ("yq", ypar, ct0), s_is_tile=True, skey="s2bc",
                    qoff=2 * G * int(bp), bpar=bp)
                for j in range(1, bs):
                    self.sc.res[("yq", ypar, ct0 + j)] = (q, [])
                if gr < G:
                    sc.op("V", lambda e, ct0=ct0, bs=bs, gr=gr, ypar=ypar,
                          yq3=yq3: e.memset(
                        yq3[:, ct0:ct0 + bs, gr:G], 0.0),
                        writes=[("yq", ypar, ct0)], deps=[q])

            def halfmap(k, h):
                if k == 0:
                    return self.psA, 512 * h, "psA"
                if k == 1:
                    return self.psB, 512 * h, "psB"
                return (self.ps1, 0, "ps1") if h == 0 else (self.psX, 0, "psX")

            def mk_mm2(gtiles=gtiles, i0=i0, ypar=ypar, gr=gr):
                nhalf = max(1, D // 512)
                wdt = min(512, D)
                for ct in range(NC):
                    wp = ct % 3
                    sc.dma(f"dT2{wp}", t2sb[wp][:, :],
                           self.t2s[128 * ct:128 * (ct + 1), :],
                           reads=[("wscr", 1)], writes=[("t2sb", wp)])
                    for (i, r0, rows) in gtiles:
                        k = i - i0
                        for half in range(nhalf):
                            ap, c0, pkey = halfmap(k, half)
                            sc.op("T", lambda e, ct=ct, half=half, k=k, wdt=wdt,
                                  ypar=ypar, wp=wp, ap=ap, c0=c0: e.matmul(
                                ap[:, c0:c0 + wdt],
                                yq[ypar][:, G * ct + 128 * k:
                                          G * ct + 128 * k + 128],
                                t2sb[wp][:, wdt * half:wdt * (half + 1)],
                                start=(ct == 0), stop=(ct == NC - 1)),
                                reads=[("yq", ypar, ct), ("t2sb", wp)],
                                writes=[pkey])
                for (i, r0, rows) in gtiles:
                    k = i - i0
                    parts = []
                    for half in range(nhalf):
                        ap, c0, pkey = halfmap(k, half)
                        parts.append((ap, c0, pkey, wdt * half, wdt))
                    x_out(i, r0, rows, parts, e2c[:, k:k + 1], ("e2c", k))
            pend_mm2[0] = mk_mm2
        if pend_mm2[0] is not None:
            pend_mm2[0]()

    def p1_ffn1(self):
        cfg, sc = self.cfg, self.sc
        D = cfg.D
        xr = [self.psb(f"x0r{i}", [128, D], F32) for i in range(9)]
        x1st = [self.psb(f"x1st{i}", [128, D], F32) for i in range(2)]
        loaded = {}

        def x_src(i, r0, rows):
            if i not in loaded:
                sc.dma(f"dX{i % 9}", xr[i % 9][:rows, :], self.x0d[r0:r0 + rows, :],
                       writes=[("x0r", i % 9)])
                loaded[i] = True
            return xr[i % 9][:rows, :], ("x0r", i % 9), ()

        def x_out(i, r0, rows, parts, esc_ap, esc_key):
            par = i % 2
            xap, xkey, _ = x_src(i, r0, rows)
            for (ap, c0, pkey, d0, w) in parts:
                sc.op("V", lambda e, rows=rows, ap=ap, c0=c0, d0=d0, w=w,
                      esc_ap=esc_ap, xap=xap: e.scalar_tensor_tensor(
                        x1st[par][:rows, d0:d0 + w], ap[:rows, c0:c0 + w],
                        esc_ap[:rows, :], xap[:rows, d0:d0 + w],
                        AL.mult, AL.add),
                    reads=[pkey, esc_key, xkey], writes=[("x1st", par)])
            sc.dma(f"dX1{par}", self.x1s[r0:r0 + rows, :], x1st[par][:rows, :],
                   reads=[("x1st", par)], writes=[("x1s", i)])

        self.ffn("f1", cfg.R1, cfg.G1, x_src, x_out)

    def p2_proj(self):
        cfg, sc = self.cfg, self.sc
        D, ND, R1, R2 = cfg.D, cfg.ND, cfg.R1, cfg.R2
        tiles = cfg.tiles(R1)

        wq = self.psb("wqsb", [128, ND * D], BF16)
        wk = self.psb("wksb", [128, ND * D], BF16)
        wv = self.psb("wvsb", [128, ND * D], BF16)
        wtmp = [self.psb(f"wld{i}", [128, D], F32) for i in range(2)]
        for m, (wd, wsb, wn) in enumerate([(self.wqd, wq, "wq"), (self.wkd, wk, "wk"),
                                           (self.wvd, wv, "wv")]):
            for fc in range(ND):
                par = (m * ND + fc) % 2
                sc.dma(f"dWp{par}", wtmp[par][:, :], wd[128 * fc:128 * (fc + 1), :],
                       writes=[("wtmp", par)])
                sc.op("V", lambda e, wsb=wsb, fc=fc, par=par: e.tensor_copy(
                    wsb[:, D * fc:D * (fc + 1)], wtmp[par][:, :]),
                    reads=[("wtmp", par)], writes=[wn])

        pb = {k: self.psb("p2" + k, [128, n], F32) for k, n in
              [("bn6", 12), ("mv", 2), ("sq", 1), ("r0", 1), ("r2", 1)]}
        u = [self.psb(f"p2u{i}", [128, D], BF16) for i in range(2)]
        ut = [self.psb(f"p2ut{i}", [128, ND * 512], BF16) for i in range(2)]
        kst = [self.psb(f"p2kst{i}", [128, 512], BF16) for i in range(2)]
        vst = [self.psb(f"p2vst{i}", [128, D], BF16) for i in range(2)]
        xr = [self.psb(f"p2x{i}", [128, D], F32) for i in range(2)]

        pgroups = []
        i = 0
        while i < len(tiles):
            pgroups.append(tiles[i:i + 4])
            i += 4

        for gi, gt in enumerate(pgroups):
            par = gi % 2
            gcols = sum(r for (_, _, r) in gt)
            for (j, (i, r0, rows)) in enumerate(gt):
                up = i % 2
                sc.dma(f"dP2x{i % 2}", xr[i % 2][:rows, :],
                       self.x1s[r0:r0 + rows, :],
                       reads=[("x1s", i)], writes=[("p2x", i % 2)])
                if rows < 128:
                    sc.op("V", lambda e, up=up: e.memset(u[up][:, :], 0.0),
                          writes=[("p2u", up)])
                self.ln_tile(pb, ("p2x", i % 2), xr[i % 2][:rows, :], rows,
                             u[up][:rows, :], ("p2u", up), False)
                for fc in range(ND):
                    sc.dma(f"dTq{fc}",
                           ut[par][:, 512 * fc + 128 * j:512 * fc + 128 * j + 128],
                           u[up][:, 128 * fc:128 * (fc + 1)],
                           reads=[("p2u", up)],
                           writes=[("ut", par, fc), ("tqo", fc)],
                           eng="S", transpose=True)
            if gcols < 512:
                for fc in range(ND):
                    sc.op("V", lambda e, fc=fc, gcols=gcols, par=par: e.memset(
                        ut[par][:, 512 * fc + gcols:512 * (fc + 1)], 0.0),
                        writes=[("ut", par, fc)])

            r0g = gt[0][1]
            qcols = max(0, min(R2 - r0g, gcols))
            for db in range(ND):
                for wsb, wn, isq in [(wq, "wq", True), (wk, "wk", False)]:
                    ncols = qcols if isq else gcols
                    if ncols == 0:
                        continue
                    pps = self.ps1 if isq else self.psC
                    ppk = "ps1" if isq else "psC"
                    for fc in range(ND):
                        sc.op("T", lambda e, db=db, fc=fc, ncols=ncols, wsb=wsb,
                              par=par, pps=pps: e.matmul(
                            pps[:, 0:ncols],
                            wsb[:, D * fc + 128 * db:D * fc + 128 * (db + 1)],
                            ut[par][:, 512 * fc:512 * fc + ncols],
                            start=(fc == 0), stop=(fc == ND - 1)),
                            reads=[wn, ("ut", par, fc)], writes=[ppk])
                    if isq:
                        sc.op("V", lambda e, db=db, r0g=r0g, ncols=ncols:
                            e.tensor_copy(
                                self.qt[:, R2 * db + r0g:R2 * db + r0g + ncols],
                                self.ps1[:, 0:ncols]),
                            reads=["ps1"], writes=[("qt", db)])
                    else:
                        kp_ = db % 2
                        sc.op("V", lambda e, ncols=ncols, kp_=kp_: e.tensor_copy(
                            kst[kp_][:, 0:ncols], self.psC[:, 0:ncols]),
                            reads=["psC"], writes=[("kst", kp_)])
                        sc.dma(f"dK{kp_}",
                               self.kscr.rearrange("(db p) r -> p db r", p=128)
                               [:, db, r0g:r0g + ncols],
                               kst[kp_][:, 0:ncols],
                               reads=[("kst", kp_)], writes=[("kscr", db)])
            for (j, (i, r0, rows)) in enumerate(gt):
                vp_ = j % 2
                mmps = self.psA if vp_ == 0 else self.psB
                mkey = "psA" if vp_ == 0 else "psB"
                nhalf = max(1, D // 512)
                wdt = min(512, D)
                for fc in range(ND):
                    for half in range(nhalf):
                        sc.op("T", lambda e, fc=fc, half=half, j=j, mmps=mmps,
                              wdt=wdt, par=par: e.matmul(
                            mmps[:, wdt * half:wdt * (half + 1)],
                            ut[par][:, 512 * fc + 128 * j:512 * fc + 128 * j + 128],
                            wv[:, D * fc + wdt * half:D * fc + wdt * (half + 1)],
                            start=(fc == 0), stop=(fc == ND - 1)),
                            reads=[("ut", par, fc), "wv"], writes=[mkey])
                sc.op("V", lambda e, rows=rows, mmps=mmps, vp_=vp_: e.tensor_copy(
                    vst[vp_][:rows, :], mmps[:rows, 0:D]),
                    reads=[mkey], writes=[("vst", vp_)])
                sc.dma(f"dV{vp_}", self.vscr[r0:r0 + rows, :], vst[vp_][:rows, :],
                       reads=[("vst", vp_)], writes=[("vscr", i)])

    def p3_attn(self):
        cfg, sc = self.cfg, self.sc
        D, H, DK, R1, R2 = cfg.D, cfg.H, cfg.DK, cfg.R1, cfg.R2
        NKT, NP, BW = cfg.NKT, cfg.NP, cfg.BW
        S = cfg.S

        band = self.psb("bandsb", [128, H * BW], F32)
        for h in range(H):
            sc.dma("dBd", band[:, BW * h:BW * (h + 1)], self.bandd[h, :, :],
                   writes=[("band", h)])
        kp = [[self.psb(f"kp{h}_{i}", [128, S], BF16) for i in range(2)]
              for h in range(2)]
        for h in range(2):
            for i in range(2):
                sc.op("V", lambda e, h=h, i=i: e.memset(kp[h][i][:, :], 0.0),
                      writes=[("kp", h, i)])
        vp = [self.psb(f"vp{i}", [128, NKT * 130], BF16) for i in range(2)]
        et = [self.psb(f"et{i}", [128, 1024], BF16) for i in range(3)]
        aot = self.psb("aotp", [128, R2], BF16)
        rrow = self.psb("rrow", [1, 1024], F32)
        dbc = self.psb("dbc", [128, 1024], F32)

        for i in range(2):
            sc.op("V", lambda e, i=i: e.memset(
                vp[i].ap().rearrange("p (kt c) -> p kt c", c=130)[:, :, 64:65], 1.0),
                writes=[("vp", i)])
            sc.op("V", lambda e, i=i: e.memset(
                vp[i].ap().rearrange("p (kt c) -> p kt c", c=130)[:, :, 129:130], 1.0),
                writes=[("vp", i)])

        def load_pair(p):
            par = p % 2
            sc.dma(f"dKp{par}", kp[0][par][0:64, :],
                   self.kscr[128 * p:128 * p + 64, 2:2 + S],
                   writes=[("kp", 0, par)])
            sc.dma(f"dKpb{par}", kp[1][par].ap()[64:128, :],
                   self.kscr[128 * p + 64:128 * (p + 1), 2:2 + S],
                   writes=[("kp", 1, par)])
            sc.dma(f"dVp{par}",
                   vp[par].ap().rearrange("p (kt c) -> p kt c", c=130)[:, :, 0:64],
                   self.vscr[2:2 + S, 128 * p:128 * p + 64].rearrange(
                       "(kt p) d -> p kt d", p=128),
                   writes=[("vp", par)])
            sc.dma(f"dVp{par}",
                   vp[par].ap().rearrange("p (kt c) -> p kt c", c=130)[:, :, 65:129],
                   self.vscr[2:2 + S, 128 * p + 64:128 * (p + 1)].rearrange(
                       "(kt p) d -> p kt d", p=128),
                   writes=[("vp", par)])

        load_pair(0)
        for p in range(NP):
            par = p % 2
            if p + 1 < NP:
                load_pair(p + 1)
            for hh in range(2):
                h = 2 * p + hh
                for (q0, qw) in cfg.QG:
                    def sparts(kt):
                        s = kt % 3
                        if s == 0:
                            return [(self.psA, "psA", 0, 1024, 0)]
                        if s == 1:
                            return [(self.psB, "psB", 0, 1024, 0)]
                        return [(self.ps1, "ps1", 0, 512, 0),
                                (self.psX, "psX", 512, 1024, 512)]

                    def score_mm(kt):
                        for (ap, key, lo, hi, off) in sparts(kt):
                            if lo >= qw:
                                break
                            aa = lo
                            bb = min(qw, hi)
                            while aa < bb:
                                b2 = min(bb, aa + 512)
                                sc.op("T", lambda e, kt=kt, aa=aa, b2=b2, hh=hh,
                                      q0=q0, ap=ap, off=off, par=par, p=p:
                                    e.matmul(
                                        ap[:, aa - off:b2 - off],
                                        kp[hh][par][:, 128 * kt:128 * (kt + 1)],
                                        self.qt.ap().rearrange(
                                            "q (db r) -> q db r", db=cfg.ND)
                                        [:, p, q0 + aa:q0 + b2],
                                        start=True, stop=True),
                                    reads=[("kp", hh, par), ("qt", p)],
                                    writes=[(key, "w")])
                                aa = b2

                    score_mm(0)
                    self._band_prep(h, 0, q0, qw, sparts(0), band)
                    if NKT > 1:
                        score_mm(1)
                        self._band_prep(h, 1, q0, qw, sparts(1), band)
                    for kt in range(NKT):
                        self._exp_regions(h, kt, q0, qw, sparts(kt),
                                          et[kt % 3], ("et", kt % 3), band)
                        if kt + 2 < NKT:
                            score_mm(kt + 2)
                            self._band_prep(h, kt + 2, q0, qw,
                                            sparts(kt + 2), band)
                        aa = 0
                        while aa < qw:
                            bb = min(qw, aa + 512)
                            sc.op("T", lambda e, kt=kt, aa=aa, bb=bb, hh=hh,
                                  par=par: e.matmul(
                                self.psC[0:65, aa:bb],
                                vp[par].ap().rearrange("p (kt c) -> p kt c",
                                                       c=130)
                                [:, kt, 65 * hh:65 * hh + 65],
                                et[kt % 3][:, aa:bb],
                                start=(kt == 0), stop=(kt == NKT - 1)),
                                reads=[("et", kt % 3), ("vp", par)],
                                writes=["psC"])
                            aa = bb
                    sc.op("V", lambda e, qw=qw: e.reciprocal(
                        rrow[0:1, 0:qw], self.psC[64:65, 0:qw]),
                        reads=["psC"], writes=["rrow"])
                    aa = 0
                    while aa < qw:
                        bb = min(qw, aa + 512)
                        sc.op("T", lambda e, aa=aa, bb=bb: e.matmul(
                            self.psA[0:64, aa:bb], self.onessb[0:1, 0:64],
                            rrow[0:1, aa:bb], start=True, stop=True),
                            reads=["rrow", "ones"],
                            writes=[("psA", "w")])
                        sc.op("V", lambda e, aa=aa, bb=bb: e.tensor_copy(
                            dbc[0:64, aa:bb], self.psA[0:64, aa:bb]),
                            reads=[("psA", "w")], writes=["dbc"])
                        aa = bb
                    sc.op("V", lambda e, qw=qw, q0=q0, hh=hh: e.tensor_tensor(
                        aot[64 * hh:64 * hh + 64, q0:q0 + qw],
                        self.psC[0:64, 0:qw], dbc[0:64, 0:qw], AL.mult),
                        reads=["psC", "dbc"], writes=[("aot", hh)])
            sc.dma("dAo", self.aots[128 * p:128 * (p + 1), :], aot[:, :],
                   reads=[("aot", 0), ("aot", 1)], writes=[("aots", p)])

    def _geom(self, kt, q0, qw):
        cfg = self.cfg
        rA = 128 * kt + 3 - cfg.BMAX
        rB = 128 * kt + 129 + cfg.BMAX
        cA = max(0, min(qw, rA - q0))
        cB = max(0, min(qw, rB - q0))
        return rA, cA, cB

    def _band_prep(self, h, kt, q0, qw, parts, band):
        cfg, sc = self.cfg, self.sc
        BW = cfg.BW
        rA, cA, cB = self._geom(kt, q0, qw)
        for (ap, key, lo, hi, off) in parts:
            a = max(max(lo, 0), cA)
            b = min(min(hi, qw), cB)
            if b > a:
                j0 = (q0 + a) - rA
                sc.op("V", lambda e, a=a, b=b, j0=j0, ap=ap, off=off:
                    e.tensor_tensor(
                        ap[:, a - off:b - off], ap[:, a - off:b - off],
                        band[:, BW * h + j0:BW * h + j0 + (b - a)], AL.add),
                    reads=[(key, "w"), ("band", h)], writes=[(key, "w")])

    def _exp_regions(self, h, kt, q0, qw, parts, etbuf, etkey, band):
        cfg, sc = self.cfg, self.sc
        rA, cA, cB = self._geom(kt, q0, qw)
        ca_ap = self.ccsb[:, h:h + 1]
        cb_ap = self.ccsb[:, cfg.H + h:cfg.H + h + 1]
        for (ap, key, lo, hi, off) in parts:
            a0, b0 = max(lo, 0), min(hi, qw)
            if b0 <= a0:
                continue
            a, b = a0, min(b0, cA)
            if b > a:
                sc.op("S", lambda e, a=a, b=b, ap=ap, off=off: e.activation(
                    etbuf[:, a:b], ap[:, a - off:b - off], AF.Exp, bias=ca_ap),
                    reads=[(key, "w"), "cc"], writes=[etkey])
            a, b = max(a0, cA), min(b0, cB)
            if b > a:
                sc.op("S", lambda e, a=a, b=b, ap=ap, off=off: e.activation(
                    etbuf[:, a:b], ap[:, a - off:b - off], AF.Exp),
                    reads=[(key, "w")], writes=[etkey])
            a, b = max(a0, cB), b0
            if b > a:
                sc.op("S", lambda e, a=a, b=b, ap=ap, off=off: e.activation(
                    etbuf[:, a:b], ap[:, a - off:b - off], AF.Exp, bias=cb_ap),
                    reads=[(key, "w"), "cc"], writes=[etkey])

    def p4_wo(self):
        cfg, sc = self.cfg, self.sc
        D, ND, R2 = cfg.D, cfg.ND, cfg.R2
        tiles = cfg.tiles(R2)

        wo = self.psb("wosb", [128, ND * D], BF16)
        wtmp = [self.psb(f"p4w{i}", [128, D], F32) for i in range(2)]
        for fc in range(ND):
            par = fc % 2
            sc.dma(f"dWp{par}", wtmp[par][:, :], self.wod[128 * fc:128 * (fc + 1), :],
                   writes=[("wtmp", par)])
            sc.op("V", lambda e, fc=fc, par=par: e.tensor_copy(
                wo[:, D * fc:D * (fc + 1)], wtmp[par][:, :]),
                reads=[("wtmp", par)], writes=["wo"])

        aotr = [self.psb(f"p4a{i}", [128, ND * 128], BF16) for i in range(2)]
        x1r = [self.psb(f"p4x{i}", [128, D], F32) for i in range(2)]
        x2st = [self.psb(f"p4o{i}", [128, D], F32) for i in range(2)]
        for i_ in range(2):
            sc.op("V", lambda e, i_=i_: e.memset(aotr[i_][:, :], 0.0),
                  writes=[("p4a", i_)])
        for (i, r0, rows) in tiles:
            par = i % 2
            sc.dma(f"dA{par}",
                   aotr[par].ap().rearrange("p (f c) -> p f c", f=ND)[:, :, 0:rows],
                   self.aots.rearrange("(f p) r -> p f r", p=128)[:, :, r0:r0 + rows],
                   writes=[("p4a", par)])
            sc.dma(f"dP4x{par}", x1r[par][:rows, :], self.x1s[r0:r0 + rows, :],
                   writes=[("p4x", par)])
            mmps = self.psA if par == 0 else self.psB
            mkey = "psA" if par == 0 else "psB"
            nhalf = max(1, D // 512)
            wdt = min(512, D)
            for fc in range(ND):
                for half in range(nhalf):
                    sc.op("T", lambda e, fc=fc, half=half, par=par, mmps=mmps,
                          wdt=wdt: e.matmul(
                        mmps[:, wdt * half:wdt * (half + 1)],
                        aotr[par][:, 128 * fc:128 * (fc + 1)],
                        wo[:, D * fc + wdt * half:D * fc + wdt * (half + 1)],
                        start=(fc == 0), stop=(fc == ND - 1)),
                        reads=[("p4a", par), "wo"], writes=[mkey])
            sc.op("V", lambda e, i=i, rows=rows, mmps=mmps, par=par:
                e.tensor_tensor(
                    x2st[par][:rows, :], mmps[:rows, 0:D],
                    x1r[par][:rows, :], AL.add),
                reads=[mkey, ("p4x", par)], writes=[("x2st", par)])
            if i == 0:
                sc.op("V", lambda e: e.memset(x2st[0][0:2, 0:D], 0.0),
                      writes=[("x2st", 0)])
            sc.dma(f"dX2o{par}", self.x2s[r0:r0 + rows, :],
                   x2st[par][:rows, :],
                   reads=[("x2st", par)], writes=[("x2s", i)])

    def p5_ffn2(self):
        cfg, sc = self.cfg, self.sc
        D = cfg.D
        xr = [self.psb(f"x2r{i}", [128, D], F32) for i in range(9)]
        outst = [self.psb(f"outst{i}", [128, D], F32) for i in range(2)]
        loaded = {}

        def x_src(i, r0, rows):
            if i not in loaded:
                sc.dma(f"dX2{i % 9}", xr[i % 9][:rows, :],
                       self.x2s[r0:r0 + rows, :],
                       reads=[("x2s", i)], writes=[("x2r", i % 9)])
                loaded[i] = True
            return xr[i % 9][:rows, :], ("x2r", i % 9), ()

        def x_out(i, r0, rows, parts, esc_ap, esc_key):
            par = i % 2
            xap, xkey, _ = x_src(i, r0, rows)
            for (ap, c0, pkey, d0, w) in parts:
                sc.op("V", lambda e, rows=rows, ap=ap, c0=c0, d0=d0, w=w,
                      esc_ap=esc_ap, xap=xap: e.scalar_tensor_tensor(
                        outst[par][:rows, d0:d0 + w], ap[:rows, c0:c0 + w],
                        esc_ap[:rows, :], xap[:rows, d0:d0 + w],
                        AL.mult, AL.add),
                    reads=[pkey, esc_key, xkey], writes=[("outst", par)])
            sc.dma(f"dOut{par}", self.outd[r0:r0 + rows, :],
                   outst[par][:rows, :],
                   reads=[("outst", par)], writes=[("out", i)])

        self.ffn("f2", cfg.R2, cfg.G2, x_src, x_out)

    def p6_out(self):
        cfg, sc = self.cfg, self.sc
        sc.op("G", lambda e: e.memset(self.stmp[:, 7:8], 0.0),
              reads=[("out", i) for (i, _, _) in cfg.tiles(cfg.R2)])


def _stage_core(cfg, core, x, w1, convw, convb, w2, wq, wk, wv, wo, table):
    b, par = core // 2, core % 2
    S, D = cfg.S, cfg.D
    xb = np.ascontiguousarray(np.asarray(x[b], dtype=np.float32))
    if par == 1:
        xb = np.ascontiguousarray(xb[::-1])
    x0 = np.zeros((cfg.R1, D), dtype=np.float32)
    x0[2:2 + S] = xb

    cw = np.asarray(convw, dtype=np.float32).reshape(cfg.C, 3)
    if par == 1:
        cw = cw[:, ::-1]
    cwl = np.zeros((128, cfg.NC, 3), dtype=np.float32)
    cbl = np.zeros((128, cfg.NC), dtype=np.float32)
    cb_ = np.asarray(convb, dtype=np.float32)
    for ct in range(cfg.NC):
        cwl[:, ct, :] = cw[128 * ct:128 * (ct + 1), :]
        cbl[:, ct] = cb_[128 * ct:128 * (ct + 1)]

    tbl = np.asarray(table, dtype=np.float32)
    p_ = np.arange(128)[:, None]
    j_ = np.arange(cfg.BW)[None, :]
    d_loc = p_ - j_ + (cfg.BMAX - 1)
    d_glob = d_loc if par == 0 else -d_loc
    bucket = _bucket_np(d_glob, cfg.NBUCK, cfg.MAXD)
    band = np.ascontiguousarray(
        np.transpose(tbl[bucket], (2, 0, 1)).astype(np.float32))

    bA = int(_bucket_np(np.array(cfg.BMAX if par == 0 else -cfg.BMAX),
                        cfg.NBUCK, cfg.MAXD))
    bB = int(_bucket_np(np.array(-cfg.BMAX if par == 0 else cfg.BMAX),
                        cfg.NBUCK, cfg.MAXD))
    cc = np.zeros((128, 2 * cfg.H), dtype=np.float32)
    cc[:, 0:cfg.H] = tbl[bA][None, :]
    cc[:, cfg.H:] = tbl[bB][None, :]

    return {
        "x0": x0,
        "w1": np.ascontiguousarray(np.asarray(w1, dtype=np.float32)),
        "w2": np.ascontiguousarray(np.asarray(w2, dtype=np.float32)),
        "cw": cwl.reshape(128, cfg.NC * 3), "cb": cbl,
        "wq": np.ascontiguousarray(np.asarray(wq, dtype=np.float32)),
        "wk": np.ascontiguousarray(np.asarray(wk, dtype=np.float32)),
        "wv": np.ascontiguousarray(np.asarray(wv, dtype=np.float32)),
        "wo": np.ascontiguousarray(np.asarray(wo, dtype=np.float32)),
        "band": band, "cc": cc,
        "ident": np.eye(128, dtype=np.float32),
        "identb": np.eye(128).astype(ml_dtypes.bfloat16),
        "ones": np.ones((128, 128), dtype=np.float32),
    }


_NC_CACHE = {}


def _get_nc(cfg):
    key = (cfg.D, cfg.C, cfg.H, cfg.S, cfg.G1, cfg.G2)
    if key not in _NC_CACHE:
        _NC_CACHE[key] = Build(cfg).build()
    return _NC_CACHE[key]


def kernel(L0_attn_hidden_states, L0_attn_Wq, L0_attn_Wk, L0_attn_Wv, L0_attn_Wo,
           L0_attn_rel_bias_table, L0_ffn_w1, L0_ffn_conv_w, L0_ffn_conv_b,
           L0_ffn_w2):
    cfg = Cfg()
    x = np.asarray(L0_attn_hidden_states)
    in_maps = [_stage_core(cfg, c, x, L0_ffn_w1, L0_ffn_conv_w, L0_ffn_conv_b,
                           L0_ffn_w2, L0_attn_Wq, L0_attn_Wk, L0_attn_Wv,
                           L0_attn_Wo, L0_attn_rel_bias_table)
               for c in range(8)]
    nc = _get_nc(cfg)
    res = run_bass_kernel_spmd(nc, in_maps, core_ids=list(range(8)))
    out = np.zeros((cfg.B, cfg.S, cfg.D), dtype=np.float32)
    half = cfg.S // 2
    for c in range(8):
        b, par = c // 2, c % 2
        o = np.asarray(res.results[c]["out"])[2:2 + half]
        if par == 0:
            out[b, :half] = o
        else:
            out[b, half:] = o[::-1]
    return out


# version 27
# speedup vs baseline: 1.1757x; 1.1757x over previous
import math
import numpy as np
import ml_dtypes
from contextlib import ExitStack

import concourse.bass as bass
import concourse.mybir as mybir
from concourse.bass_utils import run_bass_kernel_spmd

F32 = mybir.dt.float32
BF16 = mybir.dt.bfloat16
I32 = mybir.dt.int32
AL = mybir.AluOpType
AF = mybir.ActivationFunctionType
AX = mybir.AxisListType


def _bucket_scalar(d, nbuck=32, maxd=128):
    nb = nbuck // 2
    b = nb if d > 0 else 0
    ad = abs(d)
    me = nb // 2
    if ad < me:
        return b + ad
    large = me + int(math.log(max(ad, 1) / me) / math.log(maxd / me) * (nb - me))
    return b + min(large, nb - 1)


def _bucket_np(d, nbuck=32, maxd=128):
    d = np.asarray(d, dtype=np.int64)
    nb = nbuck // 2
    buckets = (d > 0).astype(np.int64) * nb
    ad = np.abs(d)
    me = nb // 2
    small = ad < me
    large = me + (np.log(np.maximum(ad, 1).astype(np.float32) / me)
                  / math.log(maxd / me) * (nb - me)).astype(np.int64)
    large = np.minimum(large, nb - 1)
    return buckets + np.where(small, ad, large)


class Cfg:
    def __init__(self, D=1024, C=4096, H=16, DK=64, S=4096, B=4,
                 G1tiles=3, G2tiles=3, NBUCK=32, MAXD=128):
        self.D, self.C, self.H, self.DK, self.S, self.B = D, C, H, DK, S, B
        self.R1 = S + 4
        self.R2 = S // 2 + 4
        self.ND = D // 128
        self.NC = C // 128
        self.NKT = S // 128
        self.NP = H // 2
        self.G1 = G1tiles * 128
        self.G2 = G2tiles * 128
        self.NBUCK, self.MAXD = NBUCK, MAXD
        nb = NBUCK // 2
        bmax = nb // 2
        while bmax < S and _bucket_scalar(bmax, NBUCK, MAXD) < NBUCK - 1:
            bmax += 1
        self.BMAX = bmax
        self.BW = 2 * self.BMAX + 126
        self.QG = []
        q = 0
        while q < self.R2:
            w = min(1024, self.R2 - q)
            self.QG.append((q, w))
            q += w

    def tiles(self, R):
        out, r, i = [], 0, 0
        while r < R:
            rows = min(128, R - r)
            out.append((i, r, rows))
            i += 1
            r += rows
        return out


class Op:
    __slots__ = ("idx", "eng", "fn", "deps", "dma_sem", "val")
    def __init__(self, idx, eng, fn, deps, dma_sem):
        self.idx, self.eng, self.fn, self.deps, self.dma_sem = \
            idx, eng, fn, deps, dma_sem
        self.val = 0


BATCH_SEMS = {"dC", "dBd"}


class Sched:
    def __init__(self):
        self.ops = []
        self.res = {}

    def op(self, eng, fn, reads=(), writes=(), deps=(), dma_sem=None):
        dd = set(d for d in deps if d is not None)
        for r in reads:
            w, _ = self.res.get(r, (None, []))
            if w is not None:
                dd.add(w)
        for wkey in writes:
            w, rs = self.res.get(wkey, (None, []))
            if w is not None:
                dd.add(w)
            dd.update(rs)
        idx = len(self.ops)
        self.ops.append(Op(idx, eng, fn, dd, dma_sem))
        for r in reads:
            w, rs = self.res.setdefault(r, (None, []))
            rs.append(idx)
        for wkey in writes:
            self.res[wkey] = (idx, [])
        return idx

    def dma(self, sem, out_ap, in_ap, reads=(), writes=(), deps=(), eng="Y",
            transpose=False):
        def fn(e, out_ap=out_ap, in_ap=in_ap, transpose=transpose):
            return e.dma_start(out=out_ap, in_=in_ap, transpose=transpose)
        return self.op(eng, fn, reads, writes, deps, dma_sem=sem)

    def barrier(self):
        last = {}
        for o in self.ops:
            if o.fn is None and o.dma_sem is None:
                continue
            last[o.eng if o.dma_sem is None else o.dma_sem] = o.idx
        deps = list(last.values())
        for eng in ("T", "V", "S", "G", "Y"):
            self.op(eng, None, deps=deps)

    def emit(self, nc, es):
        need = set()
        for o in self.ops:
            for d in o.deps:
                a = self.ops[d]
                if a.dma_sem is not None:
                    continue
                if a.eng != o.eng:
                    need.add(d)
                elif a.eng != "T":
                    need.add(d)
        sems = {}
        cnt = {}
        def sem_of(o):
            name = o.dma_sem if o.dma_sem is not None else ("e" + o.eng)
            if name not in sems:
                sems[name] = es.enter_context(nc.semaphore(name))
                cnt[name] = 0
            return name
        for o in self.ops:
            if o.dma_sem is not None:
                name = sem_of(o)
                cnt[name] += 16
                o.val = cnt[name]
            elif o.idx in need:
                name = sem_of(o)
                cnt[name] += 1
                o.val = cnt[name]
        for o in self.ops:
            if o.dma_sem in BATCH_SEMS:
                o.val = cnt[o.dma_sem]
        streams = {e: [] for e in ("T", "V", "S", "G", "Y")}
        for o in self.ops:
            streams[o.eng].append(o)
        sched = self

        def run(engobj, eng):
            waited = {}
            for o in streams[eng]:
                want = {}
                for d in o.deps:
                    a = sched.ops[d]
                    if a.eng == eng and a.dma_sem is None and eng == "T":
                        continue
                    if a.val == 0:
                        continue
                    nm = a.dma_sem if a.dma_sem is not None else ("e" + a.eng)
                    if want.get(nm, 0) < a.val:
                        want[nm] = a.val
                for nm, v in want.items():
                    if waited.get(nm, 0) >= v:
                        continue
                    waited[nm] = v
                    engobj.wait_ge(sems[nm], v)
                if o.fn is None:
                    continue
                inst = o.fn(engobj)
                if o.val and o.dma_sem is None:
                    inst.then_inc(sems["e" + o.eng], 1)
                elif o.dma_sem is not None:
                    inst.then_inc(sems[o.dma_sem], 16)

        block = es.enter_context(nc.Block())

        @block.tensor
        def _(t):
            run(t, "T")

        @block.vector
        def _(v):
            run(v, "V")

        @block.scalar
        def _(s):
            run(s, "S")

        @block.gpsimd
        def _(g):
            run(g, "G")

        @block.sync
        def _(y):
            run(y, "Y")


class Build:
    def __init__(self, cfg):
        self.cfg = cfg
        self.nc = bass.Bass("TRN2", target_bir_lowering=False, debug=False)
        self.es = ExitStack()
        self.sc = Sched()
        self._nsb = 0

    def sb(self, name, shape, dt):
        return self.es.enter_context(self.nc.sbuf_tensor(name, shape, dt))

    def build(self):
        cfg, nc = self.cfg, self.nc
        D, C, R1, R2 = cfg.D, cfg.C, cfg.R1, cfg.R2

        self.x0d = nc.dram_tensor("x0", [R1, D], F32, kind="ExternalInput").ap()
        self.w1d = nc.dram_tensor("w1", [C, D], F32, kind="ExternalInput").ap()
        self.w2d = nc.dram_tensor("w2", [D, C], F32, kind="ExternalInput").ap()
        self.cwd = nc.dram_tensor("cw", [128, cfg.NC * 3], F32, kind="ExternalInput").ap()
        self.cbd = nc.dram_tensor("cb", [128, cfg.NC], F32, kind="ExternalInput").ap()
        self.wqd = nc.dram_tensor("wq", [D, D], F32, kind="ExternalInput").ap()
        self.wkd = nc.dram_tensor("wk", [D, D], F32, kind="ExternalInput").ap()
        self.wvd = nc.dram_tensor("wv", [D, D], F32, kind="ExternalInput").ap()
        self.wod = nc.dram_tensor("wo", [D, D], F32, kind="ExternalInput").ap()
        self.bandd = nc.dram_tensor("band", [cfg.H, 128, cfg.BW], F32,
                                    kind="ExternalInput").ap()
        self.ccd = nc.dram_tensor("cc", [128, 2 * cfg.H], F32, kind="ExternalInput").ap()
        self.identd = nc.dram_tensor("ident", [128, 128], F32, kind="ExternalInput").ap()
        self.identbd = nc.dram_tensor("identb", [128, 128], BF16, kind="ExternalInput").ap()
        self.onesd = nc.dram_tensor("ones", [128, 128], F32, kind="ExternalInput").ap()
        self.outd = nc.dram_tensor("out", [R2, D], F32, kind="ExternalOutput").ap()

        self.w1ts = nc.dram_tensor("w1ts", [D, C], BF16).ap()
        self.t2s = nc.dram_tensor("t2s", [C, D], BF16).ap()
        self.x1s = nc.dram_tensor("x1s", [R1, D], F32).ap()
        self.kscr = nc.dram_tensor("kscr", [D, R1], BF16).ap()
        self.vscr = nc.dram_tensor("vscr", [R1, D], BF16).ap()
        self.aots = nc.dram_tensor("aots", [D, R2], BF16).ap()
        self.x2s = nc.dram_tensor("x2s", [R2, D], F32).ap()

        self.ident = self.sb("identsb", [128, 128], F32)
        self.identb = self.sb("identbsb", [128, 128], BF16)
        self.onessb = self.sb("onessb", [128, 128], F32)
        self.cw = self.sb("cwsb", [128, cfg.NC * 3], F32)
        self.cb = self.sb("cbsb", [128, cfg.NC], F32)
        self.ccsb = self.sb("ccsb", [128, 2 * cfg.H], F32)
        self.m1fold = self.sb("m1fold", [128, 1], F32)
        self.m2fold = self.sb("m2fold", [128, 1], F32)
        self.r1rec = self.sb("r1rec", [128, 1], F32)
        self.r2rec = self.sb("r2rec", [128, 1], F32)
        self.stmp = self.sb("stmp", [128, 8], F32)

        self.psA = self.es.enter_context(nc.psum_tensor([128, 1024], F32))
        self.psB = self.es.enter_context(nc.psum_tensor([128, 1024], F32))
        self.psC = self.es.enter_context(nc.psum_tensor([128, 1024], F32))
        self.ps1 = self.es.enter_context(nc.psum_tensor([128, 512], F32))
        self.psX = self.es.enter_context(nc.psum_tensor([128, 512], F32))

        tok0 = self.p0_consts()
        with ExitStack() as ph:
            self._ph = ph
            self.p0_weights(tok0)
        self.sc.barrier()
        with ExitStack() as ph:
            self._ph = ph
            self.p1_ffn1()
        self.sc.barrier()
        with ExitStack() as qph:
            self.qt = qph.enter_context(
                self.nc.sbuf_tensor("qtsb", [128, cfg.ND * R2], BF16))
            with ExitStack() as ph:
                self._ph = ph
                self.p2_proj()
            self.sc.barrier()
            with ExitStack() as ph:
                self._ph = ph
                self.p3_attn()
        self.sc.barrier()
        with ExitStack() as ph:
            self._ph = ph
            self.p4_wo()
        self.sc.barrier()
        with ExitStack() as ph:
            self._ph = ph
            self.p5_ffn2()
        self.p6_out()

        self.sc.emit(nc, self.es)
        self.es.close()
        return nc

    def psb(self, name, shape, dt):
        self._nsb += 1
        return self._ph.enter_context(
            self.nc.sbuf_tensor(f"{name}_{self._nsb}", shape, dt))

    def p0_consts(self):
        sc = self.sc
        sc.dma("dC", self.ident[:], self.identd[:, :], writes=["ident"])
        sc.dma("dC", self.identb[:], self.identbd[:, :], writes=["identb"])
        sc.dma("dC", self.onessb[:], self.onesd[:, :], writes=["ones"])
        sc.dma("dC", self.cw[:], self.cwd[:, :], writes=["cw"])
        sc.dma("dC", self.cb[:], self.cbd[:, :], writes=["cb"])
        return sc.dma("dC", self.ccsb[:], self.ccd[:, :], writes=["cc"])

    def p0_weights(self, tok0):
        cfg, sc = self.cfg, self.sc
        D, C = cfg.D, cfg.C

        trc = {"n": 0}

        def prep(widx, wd, n_tiles, tcols, outs, m_elems, foldmul, mfold, mrec,
                 nring):
            tag = f"w{widx}"
            wring = [self.psb(f"{tag}ring{i}", [128, tcols], F32)
                     for i in range(nring)]
            tern = [self.psb(f"{tag}tern{i}", [128, tcols], F32)
                    for i in range(2)]
            tlo = self.psb(f"{tag}tlo", [128, tcols], F32)
            tstage = [self.psb(f"{tag}stage{i}", [128, tcols], BF16)
                      for i in range(2)]
            acc = f"{tag}acc"
            accs = self.psb(f"{tag}accs", [128, 4], F32)
            nj = tcols // 128
            last_wb = {}
            sc.op("V", lambda e: e.memset(accs[:, 2:3], 0.0), writes=[acc])
            ldq = "Y" if widx == 0 else "S"
            for i in range(n_tiles):
                sc.dma(f"d{tag}p{i % nring}", wring[i % nring][:, :],
                       wd[128 * i:128 * (i + 1), :],
                       writes=[(tag, i % nring)], deps=[tok0], eng=ldq)
                sc.op("V", lambda e, i=i: e.reduce_sum(
                    accs[:, 0:1], wring[i % nring][:, :], axis=AX.X,
                    apply_absolute_value=True),
                    reads=[(tag, i % nring)], writes=[acc + "r"])
                sc.op("V", lambda e: e.tensor_tensor(
                    accs[:, 2:3], accs[:, 2:3], accs[:, 0:1], AL.add),
                    reads=[acc + "r", acc], writes=[acc])
            sc.op("T", lambda e: e.matmul(
                self.ps1[:, widx:widx + 1], self.onessb[:, :], accs[:, 2:3],
                start=True, stop=True), reads=[acc, "ones"],
                writes=[("ps1m", widx)])
            sc.op("V", lambda e: e.tensor_scalar(
                accs[:, 1:2], self.ps1[:, widx:widx + 1], 1.0 / m_elems, 1e-5,
                AL.mult, AL.max), reads=[("ps1m", widx)], writes=[acc + "m"])
            sc.op("V", lambda e: e.tensor_scalar(
                mfold[:], accs[:, 1:2], foldmul, None, AL.mult),
                reads=[acc + "m"], writes=[("mfold", widx)])
            rc = sc.op("V", lambda e: e.reciprocal(mrec[:], accs[:, 1:2]),
                       reads=[acc + "m"], writes=[("mrec", widx)])
            for i in range(n_tiles):
                sc.dma(f"d{tag}p{i % nring}", wring[i % nring][:, :],
                       wd[128 * i:128 * (i + 1), :],
                       writes=[(tag, i % nring)], eng=ldq)
                sp = i % 2
                sc.op("V", lambda e, i=i, sp=sp: e.tensor_scalar(
                    tern[sp][:, :], wring[i % nring][:, :], mrec[:], 0.5,
                    AL.mult, AL.is_gt),
                    reads=[(tag, i % nring), ("mrec", widx)],
                    writes=[(tag + "t", sp)])
                sc.op("V", lambda e, i=i: e.tensor_scalar(
                    tlo[:, :], wring[i % nring][:, :], mrec[:], -0.5,
                    AL.mult, AL.is_lt),
                    reads=[(tag, i % nring), ("mrec", widx)],
                    writes=[(tag + "lo",)])
                sc.op("V", lambda e, sp=sp: e.tensor_tensor(
                    tern[sp][:, :], tern[sp][:, :], tlo[:, :], AL.subtract),
                    reads=[(tag + "t", sp), (tag + "lo",)],
                    writes=[(tag + "t", sp)])
                for j in range(nj):
                    pp = trc["n"] % 2
                    trc["n"] += 1
                    pst = self.psA if pp == 0 else self.psB
                    sc.op("T", lambda e, j=j, sp=sp, pst=pst: e.transpose(
                        pst[:, 0:128], tern[sp][:, 128 * j:128 * (j + 1)],
                        self.ident[:, :]),
                        reads=[(tag + "t", sp), "ident"],
                        writes=[("p0ps", pp)])
                    sc.op("S", lambda e, j=j, sp=sp, pst=pst: e.copy(
                        tstage[sp][:, 128 * j:128 * (j + 1)], pst[:, 0:128]),
                        reads=[("p0ps", pp)], writes=[(tag + "s", i, j)],
                        deps=[last_wb.get(sp)])
                last_wb[sp] = sc.dma(
                    f"d{tag}o",
                    outs.rearrange("(j p) c -> p j c", p=128)
                    [:, :, 128 * i:128 * (i + 1)],
                    tstage[sp].ap().rearrange("p (j c) -> p j c", c=128),
                    reads=[(tag + "s", i, j) for j in range(nj)],
                    writes=[("wscr", widx)], eng="S")

        prep(0, self.w1d, C // 128, D, self.w1ts, float(C) * D, 1.0 / 127.0,
             self.m1fold, self.r1rec, 3)
        prep(1, self.w2d, D // 128, C, self.t2s, float(C) * D, 0.5 / 127.0,
             self.m2fold, self.r2rec, 2)

    def ln_tile(self, pb, xkey, xap, rows, out_ap, out_key, quant, skey=None,
                esc=None, mfold_key=None):
        cfg, sc = self.cfg, self.sc
        b = pb
        r = rows
        D_ = xap.shape[1]
        nh = (D_ + 511) // 512
        for j in range(nh):
            w_ = min(512, D_ - 512 * j)
            sc.op("V", lambda e, j=j, w_=w_: e.bn_stats(
                b["bn6"][:r, 6 * j:6 * j + 6], xap[:, 512 * j:512 * j + w_]),
                reads=[xkey], writes=["bn6"])
        sc.op("V", lambda e, nh=nh: e.bn_aggr(b["mv"][:r], b["bn6"][:r, :6 * nh]),
              reads=["bn6"], writes=["mv"])
        sc.op("V", lambda e: e.tensor_scalar(
            b["mv"][:r, 1:2], b["mv"][:r, 1:2], 1e-5, None, AL.add),
            reads=["mv"], writes=["mv"])
        sc.op("S", lambda e: e.sqrt(b["sq"][:r], b["mv"][:r, 1:2]),
              reads=["mv"], writes=["sq"])
        sc.op("V", lambda e: e.reciprocal(b["r0"][:r], b["sq"][:r]),
              reads=["sq"], writes=["r0"])
        sc.op("V", lambda e: e.tensor_tensor(
            b["r2"][:r], b["r0"][:r], b["r0"][:r], AL.mult),
            reads=["r0"], writes=["r2"])
        sc.op("V", lambda e: e.tensor_tensor(
            b["r2"][:r], b["r2"][:r], b["mv"][:r, 1:2], AL.mult),
            reads=["r2", "mv"], writes=["r2"])
        sc.op("V", lambda e: e.tensor_scalar(
            b["r2"][:r], b["r2"][:r], -0.5, 1.5, AL.mult, AL.add),
            reads=["r2"], writes=["r2"])
        sc.op("V", lambda e: e.tensor_tensor(
            b["r0"][:r], b["r0"][:r], b["r2"][:r], AL.mult),
            reads=["r0", "r2"], writes=["r0"])
        if not quant:
            t = sc.op("V", lambda e: e.tensor_scalar(
                out_ap, xap, b["mv"][:r, 0:1], b["r0"][:r],
                AL.subtract, AL.mult), reads=[xkey, "mv", "r0"], writes=[out_key])
            return t
        sc.op("V", lambda e: e.tensor_scalar(
            b["u"][:r], xap, b["mv"][:r, 0:1], b["r0"][:r],
            AL.subtract, AL.mult), reads=[xkey, "mv", "r0"], writes=["u"])
        sc.op("V", lambda e: e.reduce_max(
            b["s4"][:r, 0:1], b["u"][:r], axis=AX.X, apply_absolute_value=True),
            reads=["u"], writes=["s4"])
        sc.op("V", lambda e: e.tensor_scalar(
            b["s4"][:r, 1:2], b["s4"][:r, 0:1], 1e-5, None, AL.max),
            reads=["s4"], writes=["s4"])
        sc.op("V", lambda e: e.reciprocal(b["s4"][:r, 2:3], b["s4"][:r, 1:2]),
              reads=["s4"], writes=["s4"])
        sc.op("V", lambda e: e.tensor_scalar(
            b["s4"][:r, 3:4], b["s4"][:r, 2:3], 127.0, None, AL.mult),
            reads=["s4"], writes=["s4"])
        if r < 128:
            sc.op("V", lambda e: e.memset(esc[0][:, esc[1]:esc[1] + 1], 0.0),
                  writes=[esc[2]])
        et = sc.op("V", lambda e: e.tensor_scalar(
            esc[0][:r, esc[1]:esc[1] + 1], b["s4"][:r, 1:2],
            self.m1fold[:r] if mfold_key == "m1" else self.m2fold[:r], None,
            AL.mult), reads=["s4", ("mfold", 0 if mfold_key == "m1" else 1)],
            writes=[esc[2]])
        q = self.round_chain(b, "u", b["u"], b["s4"][:, 3:4], r, cfg.D,
                             out_ap, out_key)
        return q, et

    def round_chain(self, b, vkey, v_ap, s_ap, rows, width, out_ap, out_key,
                    s_is_tile=False, skey=None, qoff=0, bpar=""):
        sc = self.sc
        r, w = rows, width
        q0_, q1_ = qoff, qoff + w
        vkeys = vkey if isinstance(vkey, list) else [vkey]
        sreads = vkeys + ([skey] if skey else ["s4"])
        if s_is_tile:
            sc.op("V", lambda e: e.tensor_tensor(
                b["qa"][:r, q0_:q1_], v_ap, s_ap, AL.mult), reads=sreads,
                writes=["qa" + bpar])
            sc.op("V", lambda e: e.tensor_scalar(
                b["qi"][:r, q0_:q1_], b["qa"][:r, q0_:q1_], 256.75, None,
                AL.add), reads=["qa" + bpar], writes=["qi" + bpar])
            sc.op("V", lambda e: e.scalar_tensor_tensor(
                b["qc"][:r, q0_:q1_], b["qi"][:r, q0_:q1_], 256.5,
                b["qa"][:r, q0_:q1_], AL.subtract, AL.is_gt),
                reads=["qi" + bpar, "qa" + bpar], writes=["qc" + bpar])
        else:
            v_ap = v_ap[:r, :w]
            s_ap = s_ap[:r, :]
            sc.op("V", lambda e: e.tensor_scalar(
                b["qa"][:r, q0_:q1_], v_ap, s_ap, 256.5, AL.mult, AL.add),
                reads=sreads, writes=["qa" + bpar])
            sc.op("V", lambda e: e.tensor_scalar(
                b["qi"][:r, q0_:q1_], b["qa"][:r, q0_:q1_], 0.25, None,
                AL.add), reads=["qa" + bpar], writes=["qi" + bpar])
            sc.op("V", lambda e: e.tensor_tensor(
                b["qc"][:r, q0_:q1_], b["qi"][:r, q0_:q1_], b["qa"][:r, q0_:q1_],
                AL.is_gt), reads=["qi" + bpar, "qa" + bpar],
                writes=["qc" + bpar])
        return sc.op("V", lambda e: e.scalar_tensor_tensor(
            out_ap, b["qi"][:r, q0_:q1_], 256.0, b["qc"][:r, q0_:q1_],
            AL.subtract, AL.subtract), reads=["qi" + bpar, "qc" + bpar],
            writes=[out_key])

    def ffn(self, tag, R, G, x_src, x_out):
        cfg, sc = self.cfg, self.sc
        D, C, ND, NC = cfg.D, cfg.C, cfg.ND, cfg.NC
        Gt = G // 128
        tiles = cfg.tiles(R)
        ntiles = len(tiles)
        ngroups = (ntiles + Gt - 1) // Gt

        b = {
            "bn6": self.psb(tag + "bn6", [128, 12], F32),
            "mv": self.psb(tag + "mv", [128, 2], F32),
            "sq": self.psb(tag + "sq", [128, 1], F32),
            "r0": self.psb(tag + "r0", [128, 1], F32),
            "r2": self.psb(tag + "r2", [128, 1], F32),
            "u": self.psb(tag + "u", [128, D], F32),
            "s4": self.psb(tag + "s4", [128, 4], F32),
            "qa": self.psb(tag + "qa", [128, max(D, 4 * G)], F32),
            "qc": self.psb(tag + "qc", [128, max(D, 4 * G)], F32),
            "qi": self.psb(tag + "qi", [128, max(D, 4 * G)], I32),
        }
        b2 = b
        uqbuf = [self.psb(tag + f"uqb{i}", [128, D], BF16) for i in range(2)]
        W_ = 128 * (Gt + 2)
        gbuf = [self.psb(tag + f"gb{i}", [128, ND * W_], BF16) for i in range(3)]
        escr = self.psb(tag + "escr", [128, 3 * Gt + 3], F32)
        w1sb = [self.psb(tag + f"w1s{i}", [128, ND * 128], BF16) for i in range(3)]
        t2sb = [self.psb(tag + f"t2s{i}", [128, D], BF16) for i in range(3)]
        Hsb = [self.psb(tag + f"H{i}", [128, G + 2], F32) for i in range(2)]
        cv1 = [self.psb(tag + f"cv1{i}", [128, G], F32) for i in range(2)]
        cv2 = [self.psb(tag + f"cv2{i}", [128, G], F32) for i in range(2)]
        ysb = self.psb(tag + "y", [128, NC * G], F32)
        yab = [self.psb(tag + f"ya{i}", [128, G], F32) for i in range(2)]
        Msb = [self.psb(tag + f"M{i}", [128, G], F32) for i in range(2)]
        s2bc = self.psb(tag + "s2bc", [128, G], F32)
        e1bc = self.psb(tag + "e1bc", [128, G + 2], F32)
        a2c = self.psb(tag + "a2c", [128, Gt], F32)
        e2c = self.psb(tag + "e2c", [128, Gt], F32)
        s2c = self.psb(tag + "s2c", [128, 3], F32)
        yq = [self.psb(tag + "yq0", [128, NC * G], BF16)]

        def esc_slot(i):
            return i % (3 * Gt + 3)

        quant_done = {}

        def do_quant(i):
            if i in quant_done or i >= ntiles:
                return
            (_, r0, rows) = tiles[i]
            g = i // Gt
            par = g % 3
            k = i - g * Gt
            xap, xkey, xdeps = x_src(i, r0, rows)
            uq = uqbuf[i % 2]
            uqk = ("uqbuf", i % 2)
            if rows < 128:
                sc.op("V", lambda e, uq=uq: e.memset(uq[:, :], 0.0),
                      writes=[uqk])
            qt, et = self.ln_tile(b, xkey, xap, rows, uq[:rows, :], uqk,
                                  True, esc=(escr, esc_slot(i), ("esc", esc_slot(i))),
                                  mfold_key="m1")
            for fc in range(ND):
                targets = [(par, (k + 1) * 128)]
                if k == Gt - 1 and (g + 1) * Gt < ntiles:
                    targets.append(((g + 1) % 3, 0))
                if k == 0 and g >= 1:
                    targets.append(((g - 1) % 3, (Gt + 1) * 128))
                for (ti_, (tb, c0)) in enumerate(targets):
                    okey = ("tqo", fc) if ti_ == 0 else ("tqxo", fc)
                    sc.dma(f"dTq{fc}" if ti_ == 0 else f"dTqx{fc}",
                           gbuf[tb][:, fc * W_ + c0:fc * W_ + c0 + 128],
                           uq[:, 128 * fc:128 * (fc + 1)],
                           reads=[uqk], writes=[("gb", tb, fc), okey],
                           eng="S", transpose=True)
            quant_done[i] = True

        def edge_memsets(g):
            par = g % 3
            i0 = g * Gt
            gtl = [t for t in tiles[i0:i0 + Gt]]
            grl = sum(rr for (_, _, rr) in gtl)
            for fc in range(ND):
                base = fc * W_
                if g == 0:
                    sc.op("V", lambda e, base=base, par=par: e.memset(
                        gbuf[par][:, base + 127:base + 128], 0.0),
                        writes=[("gb", par, fc)])
                if i0 + Gt >= ntiles:
                    sc.op("V", lambda e, base=base, grl=grl, par=par: e.memset(
                        gbuf[par][:, base + 128 + grl:base + W_], 0.0),
                        writes=[("gb", par, fc)])

        edge_memsets(0)
        for (i, _, _) in tiles[0:Gt]:
            do_quant(i)
        if ngroups > 1:
            edge_memsets(1)
            for (i, _, _) in tiles[Gt:2 * Gt]:
                do_quant(i)
        pend_mm2 = [None]
        pend_q2 = [[]]
        for g in range(ngroups):
            par = g % 3
            i0 = g * Gt
            gtiles = [t for t in tiles[i0:i0 + Gt]]
            gr = sum(rr for (_, _, rr) in gtiles)

            def bc(col0, w, slot, ekey):
                aa = col0
                while aa < col0 + w:
                    bb = min(col0 + w, ((aa // 512) + 1) * 512)
                    j0 = aa - col0
                    sc.op("T", lambda e, aa=aa, bb=bb, j0=j0, slot=slot: e.matmul(
                        self.ps1[:, aa:bb],
                        escr[:, slot:slot + 1].broadcast_to([128, 128]),
                        self.ident[:, j0:j0 + (bb - aa)], start=True, stop=True),
                        reads=[ekey, "ident"], writes=["ps1"])
                    aa = bb
            if i0 - 1 >= 0:
                bc(0, 1, esc_slot(i0 - 1), ("esc", esc_slot(i0 - 1)))
            for (i, _, rows) in gtiles:
                k = i - i0
                bc(1 + 128 * k, rows, esc_slot(i), ("esc", esc_slot(i)))
            if i0 + Gt < ntiles:
                bc(1 + gr, 1, esc_slot(i0 + Gt), ("esc", esc_slot(i0 + Gt)))
            c_lo = 0 if i0 - 1 >= 0 else 1
            c_hi = gr + 2 if i0 + Gt < ntiles else gr + 1
            cp = sc.op("V", lambda e, c_lo=c_lo, c_hi=c_hi: e.tensor_copy(
                e1bc[:, c_lo:c_hi], self.ps1[:, c_lo:c_hi]),
                reads=["ps1"], writes=["e1bc"])
            if i0 - 1 < 0:
                sc.op("V", lambda e: e.memset(e1bc[:, 0:1], 0.0), writes=["e1bc"])
            if i0 + Gt >= ntiles:
                sc.op("V", lambda e, gr=gr: e.memset(e1bc[:, 1 + gr:gr + 2], 0.0),
                      writes=["e1bc"])

            sc.op("V", lambda e, gr=gr: e.memset(Msb[0][:, :gr], 0.0),
                  writes=[("M", 0)])
            sc.op("V", lambda e, gr=gr: e.memset(Msb[1][:, :gr], 0.0),
                  writes=[("M", 1)])

            def st_mm1(ct):
                wp = ct % 3
                sc.dma(f"dW1{wp}",
                       w1sb[wp].ap().rearrange("p (f c) -> p f c", f=ND)[:, :, :],
                       self.w1ts.rearrange("(fo p) c -> p fo c", p=128)
                       [:, :, 128 * ct:128 * (ct + 1)],
                       reads=[("wscr", 0)], writes=[("w1sb", wp)])
                mmoff = 0 if ct % 2 == 0 else 512
                mmps = self.psC[:, mmoff:mmoff + 512]
                mkey = "psC0" if ct % 2 == 0 else "psC1"
                for fc in range(ND):
                    base = fc * W_ + 127
                    aa = 0
                    while aa < gr + 2:
                        bb = min(gr + 2, aa + 512)
                        sc.op("T", lambda e, fc=fc, aa=aa, bb=bb, base=base,
                              mmps=mmps, wp=wp, par=par: e.matmul(
                            mmps[:, aa:bb],
                            w1sb[wp].ap().rearrange("p (f c) -> p f c", f=ND)[:, fc, :],
                            gbuf[par][:, base + aa:base + bb],
                            start=(fc == 0), stop=(fc == ND - 1)),
                            reads=[("w1sb", wp), ("gb", par, fc)],
                            writes=[mkey])
                        aa = bb

            def st_evac(ct):
                hp = ct % 2
                mmoff = 0 if ct % 2 == 0 else 512
                mmps = self.psC[:, mmoff:mmoff + 512]
                mkey = "psC0" if ct % 2 == 0 else "psC1"
                sc.op("V", lambda e, mmps=mmps, gr=gr, hp=hp: e.tensor_tensor(
                    Hsb[hp][:, 0:gr + 2], mmps[:, 0:gr + 2], e1bc[:, 0:gr + 2],
                    AL.mult), reads=[mkey, "e1bc"], writes=[("H", hp)])

            def st_tap1(ct):
                hp = ct % 2
                sc.op("S", lambda e, ct=ct, gr=gr, hp=hp: e.activation(
                    cv1[hp][:, 0:gr], Hsb[hp][:, 1:gr + 1], AF.Identity,
                    bias=self.cb[:, ct:ct + 1],
                    scale=self.cw[:, 3 * ct + 1:3 * ct + 2]),
                    reads=[("H", hp), "cw", "cb"], writes=[("cv1", hp)])

            def st_taps(ct):
                hp = ct % 2
                sc.op("V", lambda e, ct=ct, gr=gr, hp=hp: e.scalar_tensor_tensor(
                    cv2[hp][:, 0:gr], Hsb[hp][:, 0:gr],
                    self.cw[:, 3 * ct:3 * ct + 1], cv1[hp][:, 0:gr],
                    AL.mult, AL.add), reads=[("H", hp), "cw", ("cv1", hp)],
                    writes=[("cv2", hp)])
                sc.op("V", lambda e, ct=ct, gr=gr, hp=hp: e.scalar_tensor_tensor(
                    cv2[hp][:, 0:gr], Hsb[hp][:, 2:gr + 2],
                    self.cw[:, 3 * ct + 2:3 * ct + 3], cv2[hp][:, 0:gr],
                    AL.mult, AL.add), reads=[("H", hp), "cw", ("cv2", hp)],
                    writes=[("cv2", hp)])

            def st_silu(ct):
                hp = ct % 2
                sc.op("S", lambda e, ct=ct, gr=gr, hp=hp: e.activation(
                    ysb[:, G * ct:G * ct + gr], cv2[hp][:, 0:gr], AF.Silu),
                    reads=[("cv2", hp)], writes=[("y", ct)])
                sc.op("S", lambda e, ct=ct, gr=gr, hp=hp: e.activation(
                    yab[hp][:, 0:gr], ysb[:, G * ct:G * ct + gr], AF.Abs),
                    reads=[("y", ct)], writes=[("yab", hp)])

            def st_max(ct):
                hp = ct % 2
                sc.op("V", lambda e, gr=gr, hp=hp: e.tensor_tensor(
                    Msb[hp][:, 0:gr], Msb[hp][:, 0:gr], yab[hp][:, 0:gr],
                    AL.max), reads=[("yab", hp), ("M", hp)], writes=[("M", hp)])

            stages = [st_mm1, st_evac, st_tap1, st_taps, st_silu, st_max]
            for t_ in range(NC + len(stages) - 1):
                for si in reversed(range(len(stages))):
                    ct = t_ - si
                    if 0 <= ct < NC:
                        stages[si](ct)
                if t_ % 2 == 1 and pend_q2[0]:
                    pend_q2[0].pop(0)()
            while pend_q2[0]:
                pend_q2[0].pop(0)()

            if pend_mm2[0] is not None:
                pend_mm2[0]()
                pend_mm2[0] = None
            if g + 2 < ngroups:
                edge_memsets(g + 2)
                for (i, _, _) in tiles[(g + 2) * Gt:(g + 3) * Gt]:
                    do_quant(i)

            sc.op("V", lambda e, gr=gr: e.tensor_tensor(
                Msb[0][:, 0:gr], Msb[0][:, 0:gr], Msb[1][:, 0:gr], AL.max),
                reads=[("M", 0), ("M", 1)], writes=[("M", 0)])
            for (i, _, rows) in gtiles:
                k = i - i0
                sc.op("T", lambda e, k=k, rows=rows: e.transpose(
                    self.ps1[0:rows, 0:128],
                    Msb[0][:, 128 * k:128 * k + rows], self.ident[:, :]),
                    reads=[("M", 0), "ident"], writes=["ps1"])
                sc.op("V", lambda e, k=k, rows=rows: e.reduce_max(
                    a2c[:rows, k:k + 1], self.ps1[:rows, 0:128], axis=AX.X),
                    reads=["ps1"], writes=[("a2c", k)])
                sc.op("V", lambda e, k=k, rows=rows: e.tensor_scalar(
                    a2c[:rows, k:k + 1], a2c[:rows, k:k + 1], 1e-5, None, AL.max),
                    reads=[("a2c", k)], writes=[("a2c", k)])
                sc.op("V", lambda e, k=k, rows=rows: e.tensor_scalar(
                    e2c[:rows, k:k + 1], a2c[:rows, k:k + 1], self.m2fold[:rows],
                    None, AL.mult), reads=[("a2c", k), ("mfold", 1)],
                    writes=[("e2c", k)])
                sc.op("V", lambda e, k=k, rows=rows: e.reciprocal(
                    s2c[:rows, 0:1], a2c[:rows, k:k + 1]),
                    reads=[("a2c", k)], writes=["s2c"])
                if rows < 128:
                    sc.op("V", lambda e: e.memset(s2c[:, 1:2], 0.0),
                          writes=["s2c"])
                sc.op("V", lambda e, k=k, rows=rows: e.tensor_scalar(
                    s2c[:rows, 1:2], s2c[:rows, 0:1], 127.0, None, AL.mult),
                    reads=["s2c"], writes=["s2c"])
                sc.op("T", lambda e, k=k, rows=rows: e.matmul(
                    self.ps1[:, 128 + 128 * k:128 + 128 * k + rows],
                    s2c[:, 1:2].broadcast_to([128, 128]),
                    self.ident[:, 0:rows], start=True, stop=True),
                    reads=["s2c", "ident"], writes=["ps1"])
            sc.op("V", lambda e, gr=gr: e.tensor_copy(
                s2bc[:, 0:gr], self.ps1[:, 128:128 + gr]),
                reads=["ps1"], writes=["s2bc"])

            ypar = 0
            ysb3 = ysb.ap().rearrange("p (c g) -> p c g", g=G)
            yq3 = yq[ypar].ap().rearrange("p (c g) -> p c g", g=G)
            for ct0 in range(0, NC, 2):
                def q2chunk(ct0=ct0, gr=gr, ypar=ypar, yq3=yq3, ysb3=ysb3):
                    bs = min(2, NC - ct0)
                    bp = str((ct0 // 2) % 2)
                    v_ap = ysb3[:, ct0:ct0 + bs, 0:gr]
                    s_ap = s2bc[:, 0:gr].unsqueeze(1).broadcast_to(
                        [128, bs, gr])
                    o_ap = yq3[:, ct0:ct0 + bs, 0:gr]
                    ykeys = [("y", ct0 + j) for j in range(bs)]
                    q = self.round_chain(
                        b2, ykeys, v_ap, s_ap, 128, bs * gr, o_ap,
                        ("yq", ypar, ct0), s_is_tile=True, skey="s2bc",
                        qoff=2 * G * int(bp), bpar=bp)
                    for j in range(1, bs):
                        self.sc.res[("yq", ypar, ct0 + j)] = (q, [])
                    if gr < G:
                        sc.op("V", lambda e, ct0=ct0, bs=bs, gr=gr, ypar=ypar,
                              yq3=yq3: e.memset(
                            yq3[:, ct0:ct0 + bs, gr:G], 0.0),
                            writes=[("yq", ypar, ct0)], deps=[q])
                pend_q2[0].append(q2chunk)

            def halfmap(k, h):
                if k == 0:
                    return self.psA, 512 * h, "psA"
                if k == 1:
                    return self.psB, 512 * h, "psB"
                return (self.ps1, 0, "ps1") if h == 0 else (self.psX, 0, "psX")

            def mk_mm2(gtiles=gtiles, i0=i0, ypar=ypar, gr=gr):
                nhalf = max(1, D // 512)
                wdt = min(512, D)
                for ct in range(NC):
                    wp = ct % 3
                    sc.dma(f"dT2{wp}", t2sb[wp][:, :],
                           self.t2s[128 * ct:128 * (ct + 1), :],
                           reads=[("wscr", 1)], writes=[("t2sb", wp)])
                    for (i, r0, rows) in gtiles:
                        k = i - i0
                        for half in range(nhalf):
                            ap, c0, pkey = halfmap(k, half)
                            sc.op("T", lambda e, ct=ct, half=half, k=k, wdt=wdt,
                                  ypar=ypar, wp=wp, ap=ap, c0=c0: e.matmul(
                                ap[:, c0:c0 + wdt],
                                yq[ypar][:, G * ct + 128 * k:
                                          G * ct + 128 * k + 128],
                                t2sb[wp][:, wdt * half:wdt * (half + 1)],
                                start=(ct == 0), stop=(ct == NC - 1)),
                                reads=[("yq", ypar, ct), ("t2sb", wp)],
                                writes=[pkey])
                for (i, r0, rows) in gtiles:
                    k = i - i0
                    parts = []
                    for half in range(nhalf):
                        ap, c0, pkey = halfmap(k, half)
                        parts.append((ap, c0, pkey, wdt * half, wdt))
                    x_out(i, r0, rows, parts, e2c[:, k:k + 1], ("e2c", k))
            pend_mm2[0] = mk_mm2
        while pend_q2[0]:
            pend_q2[0].pop(0)()
        if pend_mm2[0] is not None:
            pend_mm2[0]()

    def p1_ffn1(self):
        cfg, sc = self.cfg, self.sc
        D = cfg.D
        xr = [self.psb(f"x0r{i}", [128, D], F32) for i in range(9)]
        x1st = [self.psb(f"x1st{i}", [128, D], F32) for i in range(2)]
        loaded = {}

        def x_src(i, r0, rows):
            if i not in loaded:
                sc.dma(f"dX{i % 9}", xr[i % 9][:rows, :], self.x0d[r0:r0 + rows, :],
                       writes=[("x0r", i % 9)])
                loaded[i] = True
            return xr[i % 9][:rows, :], ("x0r", i % 9), ()

        def x_out(i, r0, rows, parts, esc_ap, esc_key):
            par = i % 2
            xap, xkey, _ = x_src(i, r0, rows)
            for (ap, c0, pkey, d0, w) in parts:
                sc.op("V", lambda e, rows=rows, ap=ap, c0=c0, d0=d0, w=w,
                      esc_ap=esc_ap, xap=xap: e.scalar_tensor_tensor(
                        x1st[par][:rows, d0:d0 + w], ap[:rows, c0:c0 + w],
                        esc_ap[:rows, :], xap[:rows, d0:d0 + w],
                        AL.mult, AL.add),
                    reads=[pkey, esc_key, xkey], writes=[("x1st", par)])
            sc.dma(f"dX1{par}", self.x1s[r0:r0 + rows, :], x1st[par][:rows, :],
                   reads=[("x1st", par)], writes=[("x1s", i)])

        self.ffn("f1", cfg.R1, cfg.G1, x_src, x_out)

    def p2_proj(self):
        cfg, sc = self.cfg, self.sc
        D, ND, R1, R2 = cfg.D, cfg.ND, cfg.R1, cfg.R2
        tiles = cfg.tiles(R1)

        wq = self.psb("wqsb", [128, ND * D], BF16)
        wk = self.psb("wksb", [128, ND * D], BF16)
        wv = self.psb("wvsb", [128, ND * D], BF16)
        wtmp = [self.psb(f"wld{i}", [128, D], F32) for i in range(2)]
        for m, (wd, wsb, wn) in enumerate([(self.wqd, wq, "wq"), (self.wkd, wk, "wk"),
                                           (self.wvd, wv, "wv")]):
            for fc in range(ND):
                par = (m * ND + fc) % 2
                sc.dma(f"dWp{par}", wtmp[par][:, :], wd[128 * fc:128 * (fc + 1), :],
                       writes=[("wtmp", par)])
                sc.op("V", lambda e, wsb=wsb, fc=fc, par=par: e.tensor_copy(
                    wsb[:, D * fc:D * (fc + 1)], wtmp[par][:, :]),
                    reads=[("wtmp", par)], writes=[wn])

        pb = {k: self.psb("p2" + k, [128, n], F32) for k, n in
              [("bn6", 12), ("mv", 2), ("sq", 1), ("r0", 1), ("r2", 1)]}
        u = [self.psb(f"p2u{i}", [128, D], BF16) for i in range(2)]
        ut = [self.psb(f"p2ut{i}", [128, ND * 512], BF16) for i in range(2)]
        kst = [self.psb(f"p2kst{i}", [128, 512], BF16) for i in range(2)]
        vst = [self.psb(f"p2vst{i}", [128, D], BF16) for i in range(2)]
        xr = [self.psb(f"p2x{i}", [128, D], F32) for i in range(2)]

        pgroups = []
        i = 0
        while i < len(tiles):
            pgroups.append(tiles[i:i + 4])
            i += 4

        for gi, gt in enumerate(pgroups):
            par = gi % 2
            gcols = sum(r for (_, _, r) in gt)
            for (j, (i, r0, rows)) in enumerate(gt):
                up = i % 2
                sc.dma(f"dP2x{i % 2}", xr[i % 2][:rows, :],
                       self.x1s[r0:r0 + rows, :],
                       reads=[("x1s", i)], writes=[("p2x", i % 2)])
                if rows < 128:
                    sc.op("V", lambda e, up=up: e.memset(u[up][:, :], 0.0),
                          writes=[("p2u", up)])
                self.ln_tile(pb, ("p2x", i % 2), xr[i % 2][:rows, :], rows,
                             u[up][:rows, :], ("p2u", up), False)
                for fc in range(ND):
                    sc.dma(f"dTq{fc}",
                           ut[par][:, 512 * fc + 128 * j:512 * fc + 128 * j + 128],
                           u[up][:, 128 * fc:128 * (fc + 1)],
                           reads=[("p2u", up)],
                           writes=[("ut", par, fc), ("tqo", fc)],
                           eng="S", transpose=True)
            if gcols < 512:
                for fc in range(ND):
                    sc.op("V", lambda e, fc=fc, gcols=gcols, par=par: e.memset(
                        ut[par][:, 512 * fc + gcols:512 * (fc + 1)], 0.0),
                        writes=[("ut", par, fc)])

            r0g = gt[0][1]
            qcols = max(0, min(R2 - r0g, gcols))
            for db in range(ND):
                for wsb, wn, isq in [(wq, "wq", True), (wk, "wk", False)]:
                    ncols = qcols if isq else gcols
                    if ncols == 0:
                        continue
                    pps = self.ps1 if isq else self.psC
                    ppk = "ps1" if isq else "psC"
                    for fc in range(ND):
                        sc.op("T", lambda e, db=db, fc=fc, ncols=ncols, wsb=wsb,
                              par=par, pps=pps: e.matmul(
                            pps[:, 0:ncols],
                            wsb[:, D * fc + 128 * db:D * fc + 128 * (db + 1)],
                            ut[par][:, 512 * fc:512 * fc + ncols],
                            start=(fc == 0), stop=(fc == ND - 1)),
                            reads=[wn, ("ut", par, fc)], writes=[ppk])
                    if isq:
                        sc.op("V", lambda e, db=db, r0g=r0g, ncols=ncols:
                            e.tensor_copy(
                                self.qt[:, R2 * db + r0g:R2 * db + r0g + ncols],
                                self.ps1[:, 0:ncols]),
                            reads=["ps1"], writes=[("qt", db)])
                    else:
                        kp_ = db % 2
                        sc.op("V", lambda e, ncols=ncols, kp_=kp_: e.tensor_copy(
                            kst[kp_][:, 0:ncols], self.psC[:, 0:ncols]),
                            reads=["psC"], writes=[("kst", kp_)])
                        sc.dma(f"dK{kp_}",
                               self.kscr.rearrange("(db p) r -> p db r", p=128)
                               [:, db, r0g:r0g + ncols],
                               kst[kp_][:, 0:ncols],
                               reads=[("kst", kp_)], writes=[("kscr", db)])
            for (j, (i, r0, rows)) in enumerate(gt):
                vp_ = j % 2
                mmps = self.psA if vp_ == 0 else self.psB
                mkey = "psA" if vp_ == 0 else "psB"
                nhalf = max(1, D // 512)
                wdt = min(512, D)
                for fc in range(ND):
                    for half in range(nhalf):
                        sc.op("T", lambda e, fc=fc, half=half, j=j, mmps=mmps,
                              wdt=wdt, par=par: e.matmul(
                            mmps[:, wdt * half:wdt * (half + 1)],
                            ut[par][:, 512 * fc + 128 * j:512 * fc + 128 * j + 128],
                            wv[:, D * fc + wdt * half:D * fc + wdt * (half + 1)],
                            start=(fc == 0), stop=(fc == ND - 1)),
                            reads=[("ut", par, fc), "wv"], writes=[mkey])
                sc.op("V", lambda e, rows=rows, mmps=mmps, vp_=vp_: e.tensor_copy(
                    vst[vp_][:rows, :], mmps[:rows, 0:D]),
                    reads=[mkey], writes=[("vst", vp_)])
                sc.dma(f"dV{vp_}", self.vscr[r0:r0 + rows, :], vst[vp_][:rows, :],
                       reads=[("vst", vp_)], writes=[("vscr", i)])

    def p3_attn(self):
        cfg, sc = self.cfg, self.sc
        D, H, DK, R1, R2 = cfg.D, cfg.H, cfg.DK, cfg.R1, cfg.R2
        NKT, NP, BW = cfg.NKT, cfg.NP, cfg.BW
        S = cfg.S

        band = self.psb("bandsb", [128, H * BW], F32)
        for h in range(H):
            sc.dma("dBd", band[:, BW * h:BW * (h + 1)], self.bandd[h, :, :],
                   writes=[("band", h)])
        kp = [[self.psb(f"kp{h}_{i}", [128, S], BF16) for i in range(2)]
              for h in range(2)]
        for h in range(2):
            for i in range(2):
                sc.op("V", lambda e, h=h, i=i: e.memset(kp[h][i][:, :], 0.0),
                      writes=[("kp", h, i)])
        vp = [self.psb(f"vp{i}", [128, NKT * 130], BF16) for i in range(2)]
        et = [self.psb(f"et{i}", [128, 1024], BF16) for i in range(3)]
        aot = self.psb("aotp", [128, R2], BF16)
        rrow = self.psb("rrow", [1, 1024], F32)
        dbc = self.psb("dbc", [128, 1024], F32)

        for i in range(2):
            sc.op("V", lambda e, i=i: e.memset(
                vp[i].ap().rearrange("p (kt c) -> p kt c", c=130)[:, :, 64:65], 1.0),
                writes=[("vp", i)])
            sc.op("V", lambda e, i=i: e.memset(
                vp[i].ap().rearrange("p (kt c) -> p kt c", c=130)[:, :, 129:130], 1.0),
                writes=[("vp", i)])

        def load_pair(p):
            par = p % 2
            sc.dma(f"dKp{par}", kp[0][par][0:64, :],
                   self.kscr[128 * p:128 * p + 64, 2:2 + S],
                   writes=[("kp", 0, par)])
            sc.dma(f"dKpb{par}", kp[1][par].ap()[64:128, :],
                   self.kscr[128 * p + 64:128 * (p + 1), 2:2 + S],
                   writes=[("kp", 1, par)])
            sc.dma(f"dVp{par}",
                   vp[par].ap().rearrange("p (kt c) -> p kt c", c=130)[:, :, 0:64],
                   self.vscr[2:2 + S, 128 * p:128 * p + 64].rearrange(
                       "(kt p) d -> p kt d", p=128),
                   writes=[("vp", par)])
            sc.dma(f"dVp{par}",
                   vp[par].ap().rearrange("p (kt c) -> p kt c", c=130)[:, :, 65:129],
                   self.vscr[2:2 + S, 128 * p + 64:128 * (p + 1)].rearrange(
                       "(kt p) d -> p kt d", p=128),
                   writes=[("vp", par)])

        load_pair(0)
        for p in range(NP):
            par = p % 2
            if p + 1 < NP:
                load_pair(p + 1)
            for hh in range(2):
                h = 2 * p + hh
                for (q0, qw) in cfg.QG:
                    def sparts(kt):
                        s = kt % 3
                        if s == 0:
                            return [(self.psA, "psA", 0, 1024, 0)]
                        if s == 1:
                            return [(self.psB, "psB", 0, 1024, 0)]
                        return [(self.ps1, "ps1", 0, 512, 0),
                                (self.psX, "psX", 512, 1024, 512)]

                    def score_mm(kt):
                        for (ap, key, lo, hi, off) in sparts(kt):
                            if lo >= qw:
                                break
                            aa = lo
                            bb = min(qw, hi)
                            while aa < bb:
                                b2 = min(bb, aa + 512)
                                sc.op("T", lambda e, kt=kt, aa=aa, b2=b2, hh=hh,
                                      q0=q0, ap=ap, off=off, par=par, p=p:
                                    e.matmul(
                                        ap[:, aa - off:b2 - off],
                                        kp[hh][par][:, 128 * kt:128 * (kt + 1)],
                                        self.qt.ap().rearrange(
                                            "q (db r) -> q db r", db=cfg.ND)
                                        [:, p, q0 + aa:q0 + b2],
                                        start=True, stop=True),
                                    reads=[("kp", hh, par), ("qt", p)],
                                    writes=[(key, "w")])
                                aa = b2

                    score_mm(0)
                    self._band_prep(h, 0, q0, qw, sparts(0), band)
                    if NKT > 1:
                        score_mm(1)
                        self._band_prep(h, 1, q0, qw, sparts(1), band)
                    for kt in range(NKT):
                        self._exp_regions(h, kt, q0, qw, sparts(kt),
                                          et[kt % 3], ("et", kt % 3), band)
                        if kt + 2 < NKT:
                            score_mm(kt + 2)
                            self._band_prep(h, kt + 2, q0, qw,
                                            sparts(kt + 2), band)
                        aa = 0
                        while aa < qw:
                            bb = min(qw, aa + 512)
                            sc.op("T", lambda e, kt=kt, aa=aa, bb=bb, hh=hh,
                                  par=par: e.matmul(
                                self.psC[0:65, aa:bb],
                                vp[par].ap().rearrange("p (kt c) -> p kt c",
                                                       c=130)
                                [:, kt, 65 * hh:65 * hh + 65],
                                et[kt % 3][:, aa:bb],
                                start=(kt == 0), stop=(kt == NKT - 1)),
                                reads=[("et", kt % 3), ("vp", par)],
                                writes=["psC"])
                            aa = bb
                    sc.op("V", lambda e, qw=qw: e.reciprocal(
                        rrow[0:1, 0:qw], self.psC[64:65, 0:qw]),
                        reads=["psC"], writes=["rrow"])
                    aa = 0
                    while aa < qw:
                        bb = min(qw, aa + 512)
                        sc.op("T", lambda e, aa=aa, bb=bb: e.matmul(
                            self.psA[0:64, aa:bb], self.onessb[0:1, 0:64],
                            rrow[0:1, aa:bb], start=True, stop=True),
                            reads=["rrow", "ones"],
                            writes=[("psA", "w")])
                        sc.op("V", lambda e, aa=aa, bb=bb: e.tensor_copy(
                            dbc[0:64, aa:bb], self.psA[0:64, aa:bb]),
                            reads=[("psA", "w")], writes=["dbc"])
                        aa = bb
                    sc.op("V", lambda e, qw=qw, q0=q0, hh=hh: e.tensor_tensor(
                        aot[64 * hh:64 * hh + 64, q0:q0 + qw],
                        self.psC[0:64, 0:qw], dbc[0:64, 0:qw], AL.mult),
                        reads=["psC", "dbc"], writes=[("aot", hh)])
            sc.dma("dAo", self.aots[128 * p:128 * (p + 1), :], aot[:, :],
                   reads=[("aot", 0), ("aot", 1)], writes=[("aots", p)])

    def _geom(self, kt, q0, qw):
        cfg = self.cfg
        rA = 128 * kt + 3 - cfg.BMAX
        rB = 128 * kt + 129 + cfg.BMAX
        cA = max(0, min(qw, rA - q0))
        cB = max(0, min(qw, rB - q0))
        return rA, cA, cB

    def _band_prep(self, h, kt, q0, qw, parts, band):
        cfg, sc = self.cfg, self.sc
        BW = cfg.BW
        rA, cA, cB = self._geom(kt, q0, qw)
        for (ap, key, lo, hi, off) in parts:
            a = max(max(lo, 0), cA)
            b = min(min(hi, qw), cB)
            if b > a:
                j0 = (q0 + a) - rA
                sc.op("V", lambda e, a=a, b=b, j0=j0, ap=ap, off=off:
                    e.tensor_tensor(
                        ap[:, a - off:b - off], ap[:, a - off:b - off],
                        band[:, BW * h + j0:BW * h + j0 + (b - a)], AL.add),
                    reads=[(key, "w"), ("band", h)], writes=[(key, "w")])

    def _exp_regions(self, h, kt, q0, qw, parts, etbuf, etkey, band):
        cfg, sc = self.cfg, self.sc
        rA, cA, cB = self._geom(kt, q0, qw)
        ca_ap = self.ccsb[:, h:h + 1]
        cb_ap = self.ccsb[:, cfg.H + h:cfg.H + h + 1]
        for (ap, key, lo, hi, off) in parts:
            a0, b0 = max(lo, 0), min(hi, qw)
            if b0 <= a0:
                continue
            a, b = a0, min(b0, cA)
            if b > a:
                sc.op("S", lambda e, a=a, b=b, ap=ap, off=off: e.activation(
                    etbuf[:, a:b], ap[:, a - off:b - off], AF.Exp, bias=ca_ap),
                    reads=[(key, "w"), "cc"], writes=[etkey])
            a, b = max(a0, cA), min(b0, cB)
            if b > a:
                sc.op("S", lambda e, a=a, b=b, ap=ap, off=off: e.activation(
                    etbuf[:, a:b], ap[:, a - off:b - off], AF.Exp),
                    reads=[(key, "w")], writes=[etkey])
            a, b = max(a0, cB), b0
            if b > a:
                sc.op("S", lambda e, a=a, b=b, ap=ap, off=off: e.activation(
                    etbuf[:, a:b], ap[:, a - off:b - off], AF.Exp, bias=cb_ap),
                    reads=[(key, "w"), "cc"], writes=[etkey])

    def p4_wo(self):
        cfg, sc = self.cfg, self.sc
        D, ND, R2 = cfg.D, cfg.ND, cfg.R2
        tiles = cfg.tiles(R2)

        wo = self.psb("wosb", [128, ND * D], BF16)
        wtmp = [self.psb(f"p4w{i}", [128, D], F32) for i in range(2)]
        for fc in range(ND):
            par = fc % 2
            sc.dma(f"dWp{par}", wtmp[par][:, :], self.wod[128 * fc:128 * (fc + 1), :],
                   writes=[("wtmp", par)])
            sc.op("V", lambda e, fc=fc, par=par: e.tensor_copy(
                wo[:, D * fc:D * (fc + 1)], wtmp[par][:, :]),
                reads=[("wtmp", par)], writes=["wo"])

        aotr = [self.psb(f"p4a{i}", [128, ND * 128], BF16) for i in range(2)]
        x1r = [self.psb(f"p4x{i}", [128, D], F32) for i in range(2)]
        x2st = [self.psb(f"p4o{i}", [128, D], F32) for i in range(2)]
        for i_ in range(2):
            sc.op("V", lambda e, i_=i_: e.memset(aotr[i_][:, :], 0.0),
                  writes=[("p4a", i_)])
        for (i, r0, rows) in tiles:
            par = i % 2
            sc.dma(f"dA{par}",
                   aotr[par].ap().rearrange("p (f c) -> p f c", f=ND)[:, :, 0:rows],
                   self.aots.rearrange("(f p) r -> p f r", p=128)[:, :, r0:r0 + rows],
                   writes=[("p4a", par)])
            sc.dma(f"dP4x{par}", x1r[par][:rows, :], self.x1s[r0:r0 + rows, :],
                   writes=[("p4x", par)])
            mmps = self.psA if par == 0 else self.psB
            mkey = "psA" if par == 0 else "psB"
            nhalf = max(1, D // 512)
            wdt = min(512, D)
            for fc in range(ND):
                for half in range(nhalf):
                    sc.op("T", lambda e, fc=fc, half=half, par=par, mmps=mmps,
                          wdt=wdt: e.matmul(
                        mmps[:, wdt * half:wdt * (half + 1)],
                        aotr[par][:, 128 * fc:128 * (fc + 1)],
                        wo[:, D * fc + wdt * half:D * fc + wdt * (half + 1)],
                        start=(fc == 0), stop=(fc == ND - 1)),
                        reads=[("p4a", par), "wo"], writes=[mkey])
            sc.op("V", lambda e, i=i, rows=rows, mmps=mmps, par=par:
                e.tensor_tensor(
                    x2st[par][:rows, :], mmps[:rows, 0:D],
                    x1r[par][:rows, :], AL.add),
                reads=[mkey, ("p4x", par)], writes=[("x2st", par)])
            if i == 0:
                sc.op("V", lambda e: e.memset(x2st[0][0:2, 0:D], 0.0),
                      writes=[("x2st", 0)])
            sc.dma(f"dX2o{par}", self.x2s[r0:r0 + rows, :],
                   x2st[par][:rows, :],
                   reads=[("x2st", par)], writes=[("x2s", i)])

    def p5_ffn2(self):
        cfg, sc = self.cfg, self.sc
        D = cfg.D
        xr = [self.psb(f"x2r{i}", [128, D], F32) for i in range(9)]
        outst = [self.psb(f"outst{i}", [128, D], F32) for i in range(2)]
        loaded = {}

        def x_src(i, r0, rows):
            if i not in loaded:
                sc.dma(f"dX2{i % 9}", xr[i % 9][:rows, :],
                       self.x2s[r0:r0 + rows, :],
                       reads=[("x2s", i)], writes=[("x2r", i % 9)])
                loaded[i] = True
            return xr[i % 9][:rows, :], ("x2r", i % 9), ()

        def x_out(i, r0, rows, parts, esc_ap, esc_key):
            par = i % 2
            xap, xkey, _ = x_src(i, r0, rows)
            for (ap, c0, pkey, d0, w) in parts:
                sc.op("V", lambda e, rows=rows, ap=ap, c0=c0, d0=d0, w=w,
                      esc_ap=esc_ap, xap=xap: e.scalar_tensor_tensor(
                        outst[par][:rows, d0:d0 + w], ap[:rows, c0:c0 + w],
                        esc_ap[:rows, :], xap[:rows, d0:d0 + w],
                        AL.mult, AL.add),
                    reads=[pkey, esc_key, xkey], writes=[("outst", par)])
            sc.dma(f"dOut{par}", self.outd[r0:r0 + rows, :],
                   outst[par][:rows, :],
                   reads=[("outst", par)], writes=[("out", i)])

        self.ffn("f2", cfg.R2, cfg.G2, x_src, x_out)

    def p6_out(self):
        cfg, sc = self.cfg, self.sc
        sc.op("G", lambda e: e.memset(self.stmp[:, 7:8], 0.0),
              reads=[("out", i) for (i, _, _) in cfg.tiles(cfg.R2)])


def _stage_core(cfg, core, x, w1, convw, convb, w2, wq, wk, wv, wo, table):
    b, par = core // 2, core % 2
    S, D = cfg.S, cfg.D
    xb = np.ascontiguousarray(np.asarray(x[b], dtype=np.float32))
    if par == 1:
        xb = np.ascontiguousarray(xb[::-1])
    x0 = np.zeros((cfg.R1, D), dtype=np.float32)
    x0[2:2 + S] = xb

    cw = np.asarray(convw, dtype=np.float32).reshape(cfg.C, 3)
    if par == 1:
        cw = cw[:, ::-1]
    cwl = np.zeros((128, cfg.NC, 3), dtype=np.float32)
    cbl = np.zeros((128, cfg.NC), dtype=np.float32)
    cb_ = np.asarray(convb, dtype=np.float32)
    for ct in range(cfg.NC):
        cwl[:, ct, :] = cw[128 * ct:128 * (ct + 1), :]
        cbl[:, ct] = cb_[128 * ct:128 * (ct + 1)]

    tbl = np.asarray(table, dtype=np.float32)
    p_ = np.arange(128)[:, None]
    j_ = np.arange(cfg.BW)[None, :]
    d_loc = p_ - j_ + (cfg.BMAX - 1)
    d_glob = d_loc if par == 0 else -d_loc
    bucket = _bucket_np(d_glob, cfg.NBUCK, cfg.MAXD)
    band = np.ascontiguousarray(
        np.transpose(tbl[bucket], (2, 0, 1)).astype(np.float32))

    bA = int(_bucket_np(np.array(cfg.BMAX if par == 0 else -cfg.BMAX),
                        cfg.NBUCK, cfg.MAXD))
    bB = int(_bucket_np(np.array(-cfg.BMAX if par == 0 else cfg.BMAX),
                        cfg.NBUCK, cfg.MAXD))
    cc = np.zeros((128, 2 * cfg.H), dtype=np.float32)
    cc[:, 0:cfg.H] = tbl[bA][None, :]
    cc[:, cfg.H:] = tbl[bB][None, :]

    return {
        "x0": x0,
        "w1": np.ascontiguousarray(np.asarray(w1, dtype=np.float32)),
        "w2": np.ascontiguousarray(np.asarray(w2, dtype=np.float32)),
        "cw": cwl.reshape(128, cfg.NC * 3), "cb": cbl,
        "wq": np.ascontiguousarray(np.asarray(wq, dtype=np.float32)),
        "wk": np.ascontiguousarray(np.asarray(wk, dtype=np.float32)),
        "wv": np.ascontiguousarray(np.asarray(wv, dtype=np.float32)),
        "wo": np.ascontiguousarray(np.asarray(wo, dtype=np.float32)),
        "band": band, "cc": cc,
        "ident": np.eye(128, dtype=np.float32),
        "identb": np.eye(128).astype(ml_dtypes.bfloat16),
        "ones": np.ones((128, 128), dtype=np.float32),
    }


_NC_CACHE = {}


def _get_nc(cfg):
    key = (cfg.D, cfg.C, cfg.H, cfg.S, cfg.G1, cfg.G2)
    if key not in _NC_CACHE:
        _NC_CACHE[key] = Build(cfg).build()
    return _NC_CACHE[key]


def kernel(L0_attn_hidden_states, L0_attn_Wq, L0_attn_Wk, L0_attn_Wv, L0_attn_Wo,
           L0_attn_rel_bias_table, L0_ffn_w1, L0_ffn_conv_w, L0_ffn_conv_b,
           L0_ffn_w2):
    cfg = Cfg()
    x = np.asarray(L0_attn_hidden_states)
    in_maps = [_stage_core(cfg, c, x, L0_ffn_w1, L0_ffn_conv_w, L0_ffn_conv_b,
                           L0_ffn_w2, L0_attn_Wq, L0_attn_Wk, L0_attn_Wv,
                           L0_attn_Wo, L0_attn_rel_bias_table)
               for c in range(8)]
    nc = _get_nc(cfg)
    res = run_bass_kernel_spmd(nc, in_maps, core_ids=list(range(8)))
    out = np.zeros((cfg.B, cfg.S, cfg.D), dtype=np.float32)
    half = cfg.S // 2
    for c in range(8):
        b, par = c // 2, c % 2
        o = np.asarray(res.results[c]["out"])[2:2 + half]
        if par == 0:
            out[b, :half] = o
        else:
            out[b, half:] = o[::-1]
    return out


# version 28 (final)
# speedup vs baseline: 1.1977x; 1.0187x over previous
import math
import numpy as np
import ml_dtypes
from contextlib import ExitStack

import concourse.bass as bass
import concourse.mybir as mybir
from concourse.bass_utils import run_bass_kernel_spmd

F32 = mybir.dt.float32
BF16 = mybir.dt.bfloat16
I32 = mybir.dt.int32
AL = mybir.AluOpType
AF = mybir.ActivationFunctionType
AX = mybir.AxisListType


def _bucket_scalar(d, nbuck=32, maxd=128):
    nb = nbuck // 2
    b = nb if d > 0 else 0
    ad = abs(d)
    me = nb // 2
    if ad < me:
        return b + ad
    large = me + int(math.log(max(ad, 1) / me) / math.log(maxd / me) * (nb - me))
    return b + min(large, nb - 1)


def _bucket_np(d, nbuck=32, maxd=128):
    d = np.asarray(d, dtype=np.int64)
    nb = nbuck // 2
    buckets = (d > 0).astype(np.int64) * nb
    ad = np.abs(d)
    me = nb // 2
    small = ad < me
    large = me + (np.log(np.maximum(ad, 1).astype(np.float32) / me)
                  / math.log(maxd / me) * (nb - me)).astype(np.int64)
    large = np.minimum(large, nb - 1)
    return buckets + np.where(small, ad, large)


class Cfg:
    def __init__(self, D=1024, C=4096, H=16, DK=64, S=4096, B=4,
                 G1tiles=3, G2tiles=3, NBUCK=32, MAXD=128):
        self.D, self.C, self.H, self.DK, self.S, self.B = D, C, H, DK, S, B
        self.R1 = S + 4
        self.R2 = S // 2 + 4
        self.ND = D // 128
        self.NC = C // 128
        self.NKT = S // 128
        self.NP = H // 2
        self.G1 = G1tiles * 128
        self.G2 = G2tiles * 128
        self.NBUCK, self.MAXD = NBUCK, MAXD
        nb = NBUCK // 2
        bmax = nb // 2
        while bmax < S and _bucket_scalar(bmax, NBUCK, MAXD) < NBUCK - 1:
            bmax += 1
        self.BMAX = bmax
        self.BW = 2 * self.BMAX + 126
        self.QG = []
        q = 0
        while q < self.R2:
            w = min(1024, self.R2 - q)
            self.QG.append((q, w))
            q += w

    def tiles(self, R):
        out, r, i = [], 0, 0
        while r < R:
            rows = min(128, R - r)
            out.append((i, r, rows))
            i += 1
            r += rows
        return out


class Op:
    __slots__ = ("idx", "eng", "fn", "deps", "dma_sem", "val")
    def __init__(self, idx, eng, fn, deps, dma_sem):
        self.idx, self.eng, self.fn, self.deps, self.dma_sem = \
            idx, eng, fn, deps, dma_sem
        self.val = 0


BATCH_SEMS = {"dC", "dBd"}


class Sched:
    def __init__(self):
        self.ops = []
        self.res = {}

    def op(self, eng, fn, reads=(), writes=(), deps=(), dma_sem=None):
        dd = set(d for d in deps if d is not None)
        for r in reads:
            w, _ = self.res.get(r, (None, []))
            if w is not None:
                dd.add(w)
        for wkey in writes:
            w, rs = self.res.get(wkey, (None, []))
            if w is not None:
                dd.add(w)
            dd.update(rs)
        idx = len(self.ops)
        self.ops.append(Op(idx, eng, fn, dd, dma_sem))
        for r in reads:
            w, rs = self.res.setdefault(r, (None, []))
            rs.append(idx)
        for wkey in writes:
            self.res[wkey] = (idx, [])
        return idx

    def dma(self, sem, out_ap, in_ap, reads=(), writes=(), deps=(), eng="Y",
            transpose=False):
        def fn(e, out_ap=out_ap, in_ap=in_ap, transpose=transpose):
            return e.dma_start(out=out_ap, in_=in_ap, transpose=transpose)
        return self.op(eng, fn, reads, writes, deps, dma_sem=sem)

    def barrier(self):
        last = {}
        for o in self.ops:
            if o.fn is None and o.dma_sem is None:
                continue
            last[o.eng if o.dma_sem is None else o.dma_sem] = o.idx
        deps = list(last.values())
        for eng in ("T", "V", "S", "G", "Y"):
            self.op(eng, None, deps=deps)

    def emit(self, nc, es):
        need = set()
        for o in self.ops:
            for d in o.deps:
                a = self.ops[d]
                if a.dma_sem is not None:
                    continue
                if a.eng != o.eng:
                    need.add(d)
                elif a.eng != "T":
                    need.add(d)
        sems = {}
        cnt = {}
        def sem_of(o):
            name = o.dma_sem if o.dma_sem is not None else ("e" + o.eng)
            if name not in sems:
                sems[name] = es.enter_context(nc.semaphore(name))
                cnt[name] = 0
            return name
        for o in self.ops:
            if o.dma_sem is not None:
                name = sem_of(o)
                cnt[name] += 16
                o.val = cnt[name]
            elif o.idx in need:
                name = sem_of(o)
                cnt[name] += 1
                o.val = cnt[name]
        for o in self.ops:
            if o.dma_sem in BATCH_SEMS:
                o.val = cnt[o.dma_sem]
        streams = {e: [] for e in ("T", "V", "S", "G", "Y")}
        for o in self.ops:
            streams[o.eng].append(o)
        sched = self

        def run(engobj, eng):
            waited = {}
            for o in streams[eng]:
                want = {}
                for d in o.deps:
                    a = sched.ops[d]
                    if a.eng == eng and a.dma_sem is None and eng == "T":
                        continue
                    if a.val == 0:
                        continue
                    nm = a.dma_sem if a.dma_sem is not None else ("e" + a.eng)
                    if want.get(nm, 0) < a.val:
                        want[nm] = a.val
                for nm, v in want.items():
                    if waited.get(nm, 0) >= v:
                        continue
                    waited[nm] = v
                    engobj.wait_ge(sems[nm], v)
                if o.fn is None:
                    continue
                inst = o.fn(engobj)
                if o.val and o.dma_sem is None:
                    inst.then_inc(sems["e" + o.eng], 1)
                elif o.dma_sem is not None:
                    inst.then_inc(sems[o.dma_sem], 16)

        block = es.enter_context(nc.Block())

        @block.tensor
        def _(t):
            run(t, "T")

        @block.vector
        def _(v):
            run(v, "V")

        @block.scalar
        def _(s):
            run(s, "S")

        @block.gpsimd
        def _(g):
            run(g, "G")

        @block.sync
        def _(y):
            run(y, "Y")


class Build:
    def __init__(self, cfg):
        self.cfg = cfg
        self.nc = bass.Bass("TRN2", target_bir_lowering=False, debug=False)
        self.es = ExitStack()
        self.sc = Sched()
        self._nsb = 0

    def sb(self, name, shape, dt):
        return self.es.enter_context(self.nc.sbuf_tensor(name, shape, dt))

    def build(self):
        cfg, nc = self.cfg, self.nc
        D, C, R1, R2 = cfg.D, cfg.C, cfg.R1, cfg.R2

        self.x0d = nc.dram_tensor("x0", [R1, D], F32, kind="ExternalInput").ap()
        self.w1d = nc.dram_tensor("w1", [C, D], F32, kind="ExternalInput").ap()
        self.w2d = nc.dram_tensor("w2", [D, C], F32, kind="ExternalInput").ap()
        self.cwd = nc.dram_tensor("cw", [128, cfg.NC * 3], F32, kind="ExternalInput").ap()
        self.cbd = nc.dram_tensor("cb", [128, cfg.NC], F32, kind="ExternalInput").ap()
        self.wqd = nc.dram_tensor("wq", [D, D], F32, kind="ExternalInput").ap()
        self.wkd = nc.dram_tensor("wk", [D, D], F32, kind="ExternalInput").ap()
        self.wvd = nc.dram_tensor("wv", [D, D], F32, kind="ExternalInput").ap()
        self.wod = nc.dram_tensor("wo", [D, D], F32, kind="ExternalInput").ap()
        self.bandd = nc.dram_tensor("band", [cfg.H, 128, cfg.BW], F32,
                                    kind="ExternalInput").ap()
        self.ccd = nc.dram_tensor("cc", [128, 2 * cfg.H], F32, kind="ExternalInput").ap()
        self.identd = nc.dram_tensor("ident", [128, 128], F32, kind="ExternalInput").ap()
        self.identbd = nc.dram_tensor("identb", [128, 128], BF16, kind="ExternalInput").ap()
        self.onesd = nc.dram_tensor("ones", [128, 128], F32, kind="ExternalInput").ap()
        self.outd = nc.dram_tensor("out", [R2, D], F32, kind="ExternalOutput").ap()

        self.w1ts = nc.dram_tensor("w1ts", [D, C], BF16).ap()
        self.t2s = nc.dram_tensor("t2s", [C, D], BF16).ap()
        self.x1s = nc.dram_tensor("x1s", [R1, D], F32).ap()
        self.kscr = nc.dram_tensor("kscr", [D, R1], BF16).ap()
        self.vscr = nc.dram_tensor("vscr", [R1, D], BF16).ap()
        self.aots = nc.dram_tensor("aots", [D, R2], BF16).ap()
        self.x2s = nc.dram_tensor("x2s", [R2, D], F32).ap()

        self.ident = self.sb("identsb", [128, 128], F32)
        self.identb = self.sb("identbsb", [128, 128], BF16)
        self.onessb = self.sb("onessb", [128, 128], F32)
        self.cw = self.sb("cwsb", [128, cfg.NC * 3], F32)
        self.cb = self.sb("cbsb", [128, cfg.NC], F32)
        self.ccsb = self.sb("ccsb", [128, 2 * cfg.H], F32)
        self.m1fold = self.sb("m1fold", [128, 1], F32)
        self.m2fold = self.sb("m2fold", [128, 1], F32)
        self.r1rec = self.sb("r1rec", [128, 1], F32)
        self.r2rec = self.sb("r2rec", [128, 1], F32)
        self.stmp = self.sb("stmp", [128, 8], F32)

        self.psA = self.es.enter_context(nc.psum_tensor([128, 1024], F32))
        self.psB = self.es.enter_context(nc.psum_tensor([128, 1024], F32))
        self.psC = self.es.enter_context(nc.psum_tensor([128, 1024], F32))
        self.ps1 = self.es.enter_context(nc.psum_tensor([128, 512], F32))
        self.psX = self.es.enter_context(nc.psum_tensor([128, 512], F32))

        tok0 = self.p0_consts()
        with ExitStack() as ph:
            self._ph = ph
            self.p0_weights(tok0)
        self.sc.barrier()
        with ExitStack() as ph:
            self._ph = ph
            self.p1_ffn1()
        self.sc.barrier()
        with ExitStack() as qph:
            self.qt = qph.enter_context(
                self.nc.sbuf_tensor("qtsb", [128, cfg.ND * R2], BF16))
            with ExitStack() as ph:
                self._ph = ph
                self.p2_proj()
            self.sc.barrier()
            with ExitStack() as ph:
                self._ph = ph
                self.p3_attn()
        self.sc.barrier()
        with ExitStack() as ph:
            self._ph = ph
            self.p4_wo()
        self.sc.barrier()
        with ExitStack() as ph:
            self._ph = ph
            self.p5_ffn2()
        self.p6_out()

        self.sc.emit(nc, self.es)
        self.es.close()
        return nc

    def psb(self, name, shape, dt):
        self._nsb += 1
        return self._ph.enter_context(
            self.nc.sbuf_tensor(f"{name}_{self._nsb}", shape, dt))

    def p0_consts(self):
        sc = self.sc
        sc.dma("dC", self.ident[:], self.identd[:, :], writes=["ident"])
        sc.dma("dC", self.identb[:], self.identbd[:, :], writes=["identb"])
        sc.dma("dC", self.onessb[:], self.onesd[:, :], writes=["ones"])
        sc.dma("dC", self.cw[:], self.cwd[:, :], writes=["cw"])
        sc.dma("dC", self.cb[:], self.cbd[:, :], writes=["cb"])
        return sc.dma("dC", self.ccsb[:], self.ccd[:, :], writes=["cc"])

    def p0_weights(self, tok0):
        cfg, sc = self.cfg, self.sc
        D, C = cfg.D, cfg.C

        trc = {"n": 0}

        def prep(widx, wd, n_tiles, tcols, outs, m_elems, foldmul, mfold, mrec,
                 nring):
            tag = f"w{widx}"
            wring = [self.psb(f"{tag}ring{i}", [128, tcols], F32)
                     for i in range(nring)]
            tern = [self.psb(f"{tag}tern{i}", [128, tcols], F32)
                    for i in range(2)]
            tlo = self.psb(f"{tag}tlo", [128, tcols], F32)
            tstage = [self.psb(f"{tag}stage{i}", [128, tcols], BF16)
                      for i in range(2)]
            acc = f"{tag}acc"
            accs = self.psb(f"{tag}accs", [128, 4], F32)
            nj = tcols // 128
            last_wb = {}
            sc.op("V", lambda e: e.memset(accs[:, 2:3], 0.0), writes=[acc])
            ldq = "Y" if widx == 0 else "S"
            for i in range(n_tiles):
                sc.dma(f"d{tag}p{i % nring}", wring[i % nring][:, :],
                       wd[128 * i:128 * (i + 1), :],
                       writes=[(tag, i % nring)], deps=[tok0], eng=ldq)
                sc.op("V", lambda e, i=i: e.reduce_sum(
                    accs[:, 0:1], wring[i % nring][:, :], axis=AX.X,
                    apply_absolute_value=True),
                    reads=[(tag, i % nring)], writes=[acc + "r"])
                sc.op("V", lambda e: e.tensor_tensor(
                    accs[:, 2:3], accs[:, 2:3], accs[:, 0:1], AL.add),
                    reads=[acc + "r", acc], writes=[acc])
            sc.op("T", lambda e: e.matmul(
                self.ps1[:, widx:widx + 1], self.onessb[:, :], accs[:, 2:3],
                start=True, stop=True), reads=[acc, "ones"],
                writes=[("ps1m", widx)])
            sc.op("V", lambda e: e.tensor_scalar(
                accs[:, 1:2], self.ps1[:, widx:widx + 1], 1.0 / m_elems, 1e-5,
                AL.mult, AL.max), reads=[("ps1m", widx)], writes=[acc + "m"])
            sc.op("V", lambda e: e.tensor_scalar(
                mfold[:], accs[:, 1:2], foldmul, None, AL.mult),
                reads=[acc + "m"], writes=[("mfold", widx)])
            rc = sc.op("V", lambda e: e.reciprocal(mrec[:], accs[:, 1:2]),
                       reads=[acc + "m"], writes=[("mrec", widx)])
            for i in range(n_tiles):
                sc.dma(f"d{tag}p{i % nring}", wring[i % nring][:, :],
                       wd[128 * i:128 * (i + 1), :],
                       writes=[(tag, i % nring)], eng=ldq)
                sp = i % 2
                sc.op("V", lambda e, i=i, sp=sp: e.tensor_scalar(
                    tern[sp][:, :], wring[i % nring][:, :], mrec[:], 0.5,
                    AL.mult, AL.is_gt),
                    reads=[(tag, i % nring), ("mrec", widx)],
                    writes=[(tag + "t", sp)])
                sc.op("V", lambda e, i=i: e.tensor_scalar(
                    tlo[:, :], wring[i % nring][:, :], mrec[:], -0.5,
                    AL.mult, AL.is_lt),
                    reads=[(tag, i % nring), ("mrec", widx)],
                    writes=[(tag + "lo",)])
                sc.op("V", lambda e, sp=sp: e.tensor_tensor(
                    tern[sp][:, :], tern[sp][:, :], tlo[:, :], AL.subtract),
                    reads=[(tag + "t", sp), (tag + "lo",)],
                    writes=[(tag + "t", sp)])
                for j in range(nj):
                    pp = trc["n"] % 2
                    trc["n"] += 1
                    pst = self.psA if pp == 0 else self.psB
                    sc.op("T", lambda e, j=j, sp=sp, pst=pst: e.transpose(
                        pst[:, 0:128], tern[sp][:, 128 * j:128 * (j + 1)],
                        self.ident[:, :]),
                        reads=[(tag + "t", sp), "ident"],
                        writes=[("p0ps", pp)])
                    sc.op("S", lambda e, j=j, sp=sp, pst=pst: e.copy(
                        tstage[sp][:, 128 * j:128 * (j + 1)], pst[:, 0:128]),
                        reads=[("p0ps", pp)], writes=[(tag + "s", i, j)],
                        deps=[last_wb.get(sp)])
                last_wb[sp] = sc.dma(
                    f"d{tag}o",
                    outs.rearrange("(j p) c -> p j c", p=128)
                    [:, :, 128 * i:128 * (i + 1)],
                    tstage[sp].ap().rearrange("p (j c) -> p j c", c=128),
                    reads=[(tag + "s", i, j) for j in range(nj)],
                    writes=[("wscr", widx)], eng="S")

        prep(0, self.w1d, C // 128, D, self.w1ts, float(C) * D, 1.0 / 127.0,
             self.m1fold, self.r1rec, 3)
        prep(1, self.w2d, D // 128, C, self.t2s, float(C) * D, 0.5 / 127.0,
             self.m2fold, self.r2rec, 2)

    def ln_tile(self, pb, xkey, xap, rows, out_ap, out_key, quant, skey=None,
                esc=None, mfold_key=None):
        cfg, sc = self.cfg, self.sc
        b = pb
        r = rows
        D_ = xap.shape[1]
        nh = (D_ + 511) // 512
        for j in range(nh):
            w_ = min(512, D_ - 512 * j)
            sc.op("V", lambda e, j=j, w_=w_: e.bn_stats(
                b["bn6"][:r, 6 * j:6 * j + 6], xap[:, 512 * j:512 * j + w_]),
                reads=[xkey], writes=["bn6"])
        sc.op("V", lambda e, nh=nh: e.bn_aggr(b["mv"][:r], b["bn6"][:r, :6 * nh]),
              reads=["bn6"], writes=["mv"])
        sc.op("V", lambda e: e.tensor_scalar(
            b["mv"][:r, 1:2], b["mv"][:r, 1:2], 1e-5, None, AL.add),
            reads=["mv"], writes=["mv"])
        sc.op("S", lambda e: e.sqrt(b["sq"][:r], b["mv"][:r, 1:2]),
              reads=["mv"], writes=["sq"])
        sc.op("V", lambda e: e.reciprocal(b["r0"][:r], b["sq"][:r]),
              reads=["sq"], writes=["r0"])
        sc.op("V", lambda e: e.tensor_tensor(
            b["r2"][:r], b["r0"][:r], b["r0"][:r], AL.mult),
            reads=["r0"], writes=["r2"])
        sc.op("V", lambda e: e.tensor_tensor(
            b["r2"][:r], b["r2"][:r], b["mv"][:r, 1:2], AL.mult),
            reads=["r2", "mv"], writes=["r2"])
        sc.op("V", lambda e: e.tensor_scalar(
            b["r2"][:r], b["r2"][:r], -0.5, 1.5, AL.mult, AL.add),
            reads=["r2"], writes=["r2"])
        sc.op("V", lambda e: e.tensor_tensor(
            b["r0"][:r], b["r0"][:r], b["r2"][:r], AL.mult),
            reads=["r0", "r2"], writes=["r0"])
        if not quant:
            t = sc.op("V", lambda e: e.tensor_scalar(
                out_ap, xap, b["mv"][:r, 0:1], b["r0"][:r],
                AL.subtract, AL.mult), reads=[xkey, "mv", "r0"], writes=[out_key])
            return t
        sc.op("V", lambda e: e.tensor_scalar(
            b["u"][:r], xap, b["mv"][:r, 0:1], b["r0"][:r],
            AL.subtract, AL.mult), reads=[xkey, "mv", "r0"], writes=["u"])
        sc.op("V", lambda e: e.reduce_max(
            b["s4"][:r, 0:1], b["u"][:r], axis=AX.X, apply_absolute_value=True),
            reads=["u"], writes=["s4"])
        sc.op("V", lambda e: e.tensor_scalar(
            b["s4"][:r, 1:2], b["s4"][:r, 0:1], 1e-5, None, AL.max),
            reads=["s4"], writes=["s4"])
        sc.op("V", lambda e: e.reciprocal(b["s4"][:r, 2:3], b["s4"][:r, 1:2]),
              reads=["s4"], writes=["s4"])
        sc.op("V", lambda e: e.tensor_scalar(
            b["s4"][:r, 3:4], b["s4"][:r, 2:3], 127.0, None, AL.mult),
            reads=["s4"], writes=["s4"])
        if r < 128:
            sc.op("V", lambda e: e.memset(esc[0][:, esc[1]:esc[1] + 1], 0.0),
                  writes=[esc[2]])
        et = sc.op("V", lambda e: e.tensor_scalar(
            esc[0][:r, esc[1]:esc[1] + 1], b["s4"][:r, 1:2],
            self.m1fold[:r] if mfold_key == "m1" else self.m2fold[:r], None,
            AL.mult), reads=["s4", ("mfold", 0 if mfold_key == "m1" else 1)],
            writes=[esc[2]])
        q = self.round_chain(b, "u", b["u"], b["s4"][:, 3:4], r, cfg.D,
                             out_ap, out_key)
        return q, et

    def round_chain(self, b, vkey, v_ap, s_ap, rows, width, out_ap, out_key,
                    s_is_tile=False, skey=None, qoff=0, bpar=""):
        sc = self.sc
        r, w = rows, width
        q0_, q1_ = qoff, qoff + w
        vkeys = vkey if isinstance(vkey, list) else [vkey]
        sreads = vkeys + ([skey] if skey else ["s4"])
        if s_is_tile:
            sc.op("V", lambda e: e.tensor_tensor(
                b["qa"][:r, q0_:q1_], v_ap, s_ap, AL.mult), reads=sreads,
                writes=["qa" + bpar])
            sc.op("V", lambda e: e.tensor_scalar(
                b["qi"][:r, q0_:q1_], b["qa"][:r, q0_:q1_], 256.75, None,
                AL.add), reads=["qa" + bpar], writes=["qi" + bpar])
            sc.op("V", lambda e: e.scalar_tensor_tensor(
                b["qc"][:r, q0_:q1_], b["qi"][:r, q0_:q1_], 256.5,
                b["qa"][:r, q0_:q1_], AL.subtract, AL.is_gt),
                reads=["qi" + bpar, "qa" + bpar], writes=["qc" + bpar])
        else:
            v_ap = v_ap[:r, :w]
            s_ap = s_ap[:r, :]
            sc.op("V", lambda e: e.tensor_scalar(
                b["qa"][:r, q0_:q1_], v_ap, s_ap, 256.5, AL.mult, AL.add),
                reads=sreads, writes=["qa" + bpar])
            sc.op("V", lambda e: e.tensor_scalar(
                b["qi"][:r, q0_:q1_], b["qa"][:r, q0_:q1_], 0.25, None,
                AL.add), reads=["qa" + bpar], writes=["qi" + bpar])
            sc.op("V", lambda e: e.tensor_tensor(
                b["qc"][:r, q0_:q1_], b["qi"][:r, q0_:q1_], b["qa"][:r, q0_:q1_],
                AL.is_gt), reads=["qi" + bpar, "qa" + bpar],
                writes=["qc" + bpar])
        return sc.op("V", lambda e: e.scalar_tensor_tensor(
            out_ap, b["qi"][:r, q0_:q1_], 256.0, b["qc"][:r, q0_:q1_],
            AL.subtract, AL.subtract), reads=["qi" + bpar, "qc" + bpar],
            writes=[out_key])

    def ffn(self, tag, R, G, x_src, x_out):
        cfg, sc = self.cfg, self.sc
        D, C, ND, NC = cfg.D, cfg.C, cfg.ND, cfg.NC
        Gt = G // 128
        tiles = cfg.tiles(R)
        ntiles = len(tiles)
        ngroups = (ntiles + Gt - 1) // Gt

        b = {
            "bn6": self.psb(tag + "bn6", [128, 12], F32),
            "mv": self.psb(tag + "mv", [128, 2], F32),
            "sq": self.psb(tag + "sq", [128, 1], F32),
            "r0": self.psb(tag + "r0", [128, 1], F32),
            "r2": self.psb(tag + "r2", [128, 1], F32),
            "u": self.psb(tag + "u", [128, D], F32),
            "s4": self.psb(tag + "s4", [128, 4], F32),
            "qa": self.psb(tag + "qa", [128, max(D, 4 * G)], F32),
            "qc": self.psb(tag + "qc", [128, max(D, 4 * G)], F32),
            "qi": self.psb(tag + "qi", [128, max(D, 4 * G)], I32),
        }
        b2 = b
        uqbuf = [self.psb(tag + f"uqb{i}", [128, D], BF16) for i in range(2)]
        W_ = 128 * (Gt + 2)
        gbuf = [self.psb(tag + f"gb{i}", [128, ND * W_], BF16) for i in range(3)]
        escr = self.psb(tag + "escr", [128, 3 * Gt + 3], F32)
        w1sb = [self.psb(tag + f"w1s{i}", [128, ND * 128], BF16) for i in range(3)]
        t2sb = [self.psb(tag + f"t2s{i}", [128, D], BF16) for i in range(3)]
        Hsb = [self.psb(tag + f"H{i}", [128, G + 2], F32) for i in range(2)]
        cv1 = [self.psb(tag + f"cv1{i}", [128, G], F32) for i in range(2)]
        cv2 = [self.psb(tag + f"cv2{i}", [128, G], F32) for i in range(2)]
        ysb = self.psb(tag + "y", [128, NC * G], F32)
        yab = [self.psb(tag + f"ya{i}", [128, G], F32) for i in range(2)]
        Msb = [self.psb(tag + f"M{i}", [128, G], F32) for i in range(2)]
        s2bc = self.psb(tag + "s2bc", [128, G], F32)
        e1bc = self.psb(tag + "e1bc", [128, G + 2], F32)
        a2c = self.psb(tag + "a2c", [128, Gt], F32)
        e2c = self.psb(tag + "e2c", [128, Gt], F32)
        s2c = self.psb(tag + "s2c", [128, 3], F32)
        yq = [self.psb(tag + "yq0", [128, NC * G], BF16)]

        def esc_slot(i):
            return i % (3 * Gt + 3)

        quant_done = {}

        def do_quant(i):
            if i in quant_done or i >= ntiles:
                return
            (_, r0, rows) = tiles[i]
            g = i // Gt
            par = g % 3
            k = i - g * Gt
            xap, xkey, xdeps = x_src(i, r0, rows)
            uq = uqbuf[i % 2]
            uqk = ("uqbuf", i % 2)
            if rows < 128:
                sc.op("V", lambda e, uq=uq: e.memset(uq[:, :], 0.0),
                      writes=[uqk])
            qt, et = self.ln_tile(b, xkey, xap, rows, uq[:rows, :], uqk,
                                  True, esc=(escr, esc_slot(i), ("esc", esc_slot(i))),
                                  mfold_key="m1")
            for fc in range(ND):
                targets = [(par, (k + 1) * 128)]
                if k == Gt - 1 and (g + 1) * Gt < ntiles:
                    targets.append(((g + 1) % 3, 0))
                if k == 0 and g >= 1:
                    targets.append(((g - 1) % 3, (Gt + 1) * 128))
                for (ti_, (tb, c0)) in enumerate(targets):
                    okey = ("tqo", fc) if ti_ == 0 else ("tqxo", fc)
                    sc.dma(f"dTq{fc}" if ti_ == 0 else f"dTqx{fc}",
                           gbuf[tb][:, fc * W_ + c0:fc * W_ + c0 + 128],
                           uq[:, 128 * fc:128 * (fc + 1)],
                           reads=[uqk], writes=[("gb", tb, fc), okey],
                           eng="S", transpose=True)
            quant_done[i] = True

        def edge_memsets(g):
            par = g % 3
            i0 = g * Gt
            gtl = [t for t in tiles[i0:i0 + Gt]]
            grl = sum(rr for (_, _, rr) in gtl)
            for fc in range(ND):
                base = fc * W_
                if g == 0:
                    sc.op("V", lambda e, base=base, par=par: e.memset(
                        gbuf[par][:, base + 127:base + 128], 0.0),
                        writes=[("gb", par, fc)])
                if i0 + Gt >= ntiles:
                    sc.op("V", lambda e, base=base, grl=grl, par=par: e.memset(
                        gbuf[par][:, base + 128 + grl:base + W_], 0.0),
                        writes=[("gb", par, fc)])

        edge_memsets(0)
        for (i, _, _) in tiles[0:Gt]:
            do_quant(i)
        if ngroups > 1:
            edge_memsets(1)
            for (i, _, _) in tiles[Gt:2 * Gt]:
                do_quant(i)
        pend_mm2 = [None]
        pend_q2 = [[]]
        for g in range(ngroups):
            par = g % 3
            i0 = g * Gt
            gtiles = [t for t in tiles[i0:i0 + Gt]]
            gr = sum(rr for (_, _, rr) in gtiles)

            def bc(col0, w, slot, ekey):
                aa = col0
                while aa < col0 + w:
                    bb = min(col0 + w, ((aa // 512) + 1) * 512)
                    j0 = aa - col0
                    sc.op("T", lambda e, aa=aa, bb=bb, j0=j0, slot=slot: e.matmul(
                        self.ps1[:, aa:bb],
                        escr[:, slot:slot + 1].broadcast_to([128, 128]),
                        self.ident[:, j0:j0 + (bb - aa)], start=True, stop=True),
                        reads=[ekey, "ident"], writes=["ps1"])
                    aa = bb
            if i0 - 1 >= 0:
                bc(0, 1, esc_slot(i0 - 1), ("esc", esc_slot(i0 - 1)))
            for (i, _, rows) in gtiles:
                k = i - i0
                bc(1 + 128 * k, rows, esc_slot(i), ("esc", esc_slot(i)))
            if i0 + Gt < ntiles:
                bc(1 + gr, 1, esc_slot(i0 + Gt), ("esc", esc_slot(i0 + Gt)))
            c_lo = 0 if i0 - 1 >= 0 else 1
            c_hi = gr + 2 if i0 + Gt < ntiles else gr + 1
            cp = sc.op("V", lambda e, c_lo=c_lo, c_hi=c_hi: e.tensor_copy(
                e1bc[:, c_lo:c_hi], self.ps1[:, c_lo:c_hi]),
                reads=["ps1"], writes=["e1bc"])
            if i0 - 1 < 0:
                sc.op("V", lambda e: e.memset(e1bc[:, 0:1], 0.0), writes=["e1bc"])
            if i0 + Gt >= ntiles:
                sc.op("V", lambda e, gr=gr: e.memset(e1bc[:, 1 + gr:gr + 2], 0.0),
                      writes=["e1bc"])

            sc.op("V", lambda e, gr=gr: e.memset(Msb[0][:, :gr], 0.0),
                  writes=[("M", 0)])
            sc.op("V", lambda e, gr=gr: e.memset(Msb[1][:, :gr], 0.0),
                  writes=[("M", 1)])

            def st_mm1(ct):
                wp = ct % 3
                sc.dma(f"dW1{wp}",
                       w1sb[wp].ap().rearrange("p (f c) -> p f c", f=ND)[:, :, :],
                       self.w1ts.rearrange("(fo p) c -> p fo c", p=128)
                       [:, :, 128 * ct:128 * (ct + 1)],
                       reads=[("wscr", 0)], writes=[("w1sb", wp)])
                mmoff = 0 if ct % 2 == 0 else 512
                mmps = self.psC[:, mmoff:mmoff + 512]
                mkey = "psC0" if ct % 2 == 0 else "psC1"
                for fc in range(ND):
                    base = fc * W_ + 127
                    aa = 0
                    while aa < gr + 2:
                        bb = min(gr + 2, aa + 512)
                        sc.op("T", lambda e, fc=fc, aa=aa, bb=bb, base=base,
                              mmps=mmps, wp=wp, par=par: e.matmul(
                            mmps[:, aa:bb],
                            w1sb[wp].ap().rearrange("p (f c) -> p f c", f=ND)[:, fc, :],
                            gbuf[par][:, base + aa:base + bb],
                            start=(fc == 0), stop=(fc == ND - 1)),
                            reads=[("w1sb", wp), ("gb", par, fc)],
                            writes=[mkey])
                        aa = bb

            def st_evac(ct):
                hp = ct % 2
                mmoff = 0 if ct % 2 == 0 else 512
                mmps = self.psC[:, mmoff:mmoff + 512]
                mkey = "psC0" if ct % 2 == 0 else "psC1"
                sc.op("V", lambda e, mmps=mmps, gr=gr, hp=hp: e.tensor_tensor(
                    Hsb[hp][:, 0:gr + 2], mmps[:, 0:gr + 2], e1bc[:, 0:gr + 2],
                    AL.mult), reads=[mkey, "e1bc"], writes=[("H", hp)])

            def st_tap1(ct):
                hp = ct % 2
                sc.op("S", lambda e, ct=ct, gr=gr, hp=hp: e.activation(
                    cv1[hp][:, 0:gr], Hsb[hp][:, 1:gr + 1], AF.Identity,
                    bias=self.cb[:, ct:ct + 1],
                    scale=self.cw[:, 3 * ct + 1:3 * ct + 2]),
                    reads=[("H", hp), "cw", "cb"], writes=[("cv1", hp)])

            def st_taps(ct):
                hp = ct % 2
                sc.op("V", lambda e, ct=ct, gr=gr, hp=hp: e.scalar_tensor_tensor(
                    cv2[hp][:, 0:gr], Hsb[hp][:, 0:gr],
                    self.cw[:, 3 * ct:3 * ct + 1], cv1[hp][:, 0:gr],
                    AL.mult, AL.add), reads=[("H", hp), "cw", ("cv1", hp)],
                    writes=[("cv2", hp)])
                sc.op("V", lambda e, ct=ct, gr=gr, hp=hp: e.scalar_tensor_tensor(
                    cv2[hp][:, 0:gr], Hsb[hp][:, 2:gr + 2],
                    self.cw[:, 3 * ct + 2:3 * ct + 3], cv2[hp][:, 0:gr],
                    AL.mult, AL.add), reads=[("H", hp), "cw", ("cv2", hp)],
                    writes=[("cv2", hp)])

            def st_silu(ct):
                hp = ct % 2
                sc.op("S", lambda e, ct=ct, gr=gr, hp=hp: e.activation(
                    ysb[:, G * ct:G * ct + gr], cv2[hp][:, 0:gr], AF.Silu),
                    reads=[("cv2", hp)], writes=[("y", ct)])
                sc.op("S", lambda e, ct=ct, gr=gr, hp=hp: e.activation(
                    yab[hp][:, 0:gr], ysb[:, G * ct:G * ct + gr], AF.Abs),
                    reads=[("y", ct)], writes=[("yab", hp)])

            def st_max(ct):
                hp = ct % 2
                sc.op("V", lambda e, gr=gr, hp=hp: e.tensor_tensor(
                    Msb[hp][:, 0:gr], Msb[hp][:, 0:gr], yab[hp][:, 0:gr],
                    AL.max), reads=[("yab", hp), ("M", hp)], writes=[("M", hp)])

            stages = [st_mm1, st_evac, st_tap1, st_taps, st_silu, st_max]
            for t_ in range(NC + len(stages) - 1):
                for si in reversed(range(len(stages))):
                    ct = t_ - si
                    if 0 <= ct < NC:
                        stages[si](ct)
                if t_ % 2 == 1 and pend_q2[0]:
                    pend_q2[0].pop(0)()
            while pend_q2[0]:
                pend_q2[0].pop(0)()

            if pend_mm2[0] is not None:
                pend_mm2[0]()
                pend_mm2[0] = None
            if g + 2 < ngroups:
                edge_memsets(g + 2)
                for (i, _, _) in tiles[(g + 2) * Gt:(g + 3) * Gt]:
                    do_quant(i)

            sc.op("V", lambda e, gr=gr: e.tensor_tensor(
                Msb[0][:, 0:gr], Msb[0][:, 0:gr], Msb[1][:, 0:gr], AL.max),
                reads=[("M", 0), ("M", 1)], writes=[("M", 0)])
            for (i, _, rows) in gtiles:
                k = i - i0
                sc.op("T", lambda e, k=k, rows=rows: e.transpose(
                    self.ps1[0:rows, 0:128],
                    Msb[0][:, 128 * k:128 * k + rows], self.ident[:, :]),
                    reads=[("M", 0), "ident"], writes=["ps1"])
                sc.op("V", lambda e, k=k, rows=rows: e.reduce_max(
                    a2c[:rows, k:k + 1], self.ps1[:rows, 0:128], axis=AX.X),
                    reads=["ps1"], writes=[("a2c", k)])
                sc.op("V", lambda e, k=k, rows=rows: e.tensor_scalar(
                    a2c[:rows, k:k + 1], a2c[:rows, k:k + 1], 1e-5, None, AL.max),
                    reads=[("a2c", k)], writes=[("a2c", k)])
                sc.op("V", lambda e, k=k, rows=rows: e.tensor_scalar(
                    e2c[:rows, k:k + 1], a2c[:rows, k:k + 1], self.m2fold[:rows],
                    None, AL.mult), reads=[("a2c", k), ("mfold", 1)],
                    writes=[("e2c", k)])
                sc.op("V", lambda e, k=k, rows=rows: e.reciprocal(
                    s2c[:rows, 0:1], a2c[:rows, k:k + 1]),
                    reads=[("a2c", k)], writes=["s2c"])
                if rows < 128:
                    sc.op("V", lambda e: e.memset(s2c[:, 1:2], 0.0),
                          writes=["s2c"])
                sc.op("V", lambda e, k=k, rows=rows: e.tensor_scalar(
                    s2c[:rows, 1:2], s2c[:rows, 0:1], 127.0, None, AL.mult),
                    reads=["s2c"], writes=["s2c"])
                sc.op("T", lambda e, k=k, rows=rows: e.matmul(
                    self.ps1[:, 128 + 128 * k:128 + 128 * k + rows],
                    s2c[:, 1:2].broadcast_to([128, 128]),
                    self.ident[:, 0:rows], start=True, stop=True),
                    reads=["s2c", "ident"], writes=["ps1"])
            sc.op("V", lambda e, gr=gr: e.tensor_copy(
                s2bc[:, 0:gr], self.ps1[:, 128:128 + gr]),
                reads=["ps1"], writes=["s2bc"])

            ypar = 0
            ysb3 = ysb.ap().rearrange("p (c g) -> p c g", g=G)
            yq3 = yq[ypar].ap().rearrange("p (c g) -> p c g", g=G)
            for ct0 in range(0, NC, 2):
                def q2chunk(ct0=ct0, gr=gr, ypar=ypar, yq3=yq3, ysb3=ysb3):
                    bs = min(2, NC - ct0)
                    bp = str((ct0 // 2) % 2)
                    v_ap = ysb3[:, ct0:ct0 + bs, 0:gr]
                    s_ap = s2bc[:, 0:gr].unsqueeze(1).broadcast_to(
                        [128, bs, gr])
                    o_ap = yq3[:, ct0:ct0 + bs, 0:gr]
                    ykeys = [("y", ct0 + j) for j in range(bs)]
                    q = self.round_chain(
                        b2, ykeys, v_ap, s_ap, 128, bs * gr, o_ap,
                        ("yq", ypar, ct0), s_is_tile=True, skey="s2bc",
                        qoff=2 * G * int(bp), bpar=bp)
                    for j in range(1, bs):
                        self.sc.res[("yq", ypar, ct0 + j)] = (q, [])
                    if gr < G:
                        sc.op("V", lambda e, ct0=ct0, bs=bs, gr=gr, ypar=ypar,
                              yq3=yq3: e.memset(
                            yq3[:, ct0:ct0 + bs, gr:G], 0.0),
                            writes=[("yq", ypar, ct0)], deps=[q])
                pend_q2[0].append(q2chunk)

            def halfmap(k, h):
                if k == 0:
                    return self.psA, 512 * h, "psA"
                if k == 1:
                    return self.psB, 512 * h, "psB"
                return (self.ps1, 0, "ps1") if h == 0 else (self.psX, 0, "psX")

            def mk_mm2(gtiles=gtiles, i0=i0, ypar=ypar, gr=gr):
                nhalf = max(1, D // 512)
                wdt = min(512, D)
                for ct in range(NC):
                    wp = ct % 3
                    sc.dma(f"dT2{wp}", t2sb[wp][:, :],
                           self.t2s[128 * ct:128 * (ct + 1), :],
                           reads=[("wscr", 1)], writes=[("t2sb", wp)])
                    for (i, r0, rows) in gtiles:
                        k = i - i0
                        for half in range(nhalf):
                            ap, c0, pkey = halfmap(k, half)
                            sc.op("T", lambda e, ct=ct, half=half, k=k, wdt=wdt,
                                  ypar=ypar, wp=wp, ap=ap, c0=c0: e.matmul(
                                ap[:, c0:c0 + wdt],
                                yq[ypar][:, G * ct + 128 * k:
                                          G * ct + 128 * k + 128],
                                t2sb[wp][:, wdt * half:wdt * (half + 1)],
                                start=(ct == 0), stop=(ct == NC - 1)),
                                reads=[("yq", ypar, ct), ("t2sb", wp)],
                                writes=[pkey])
                for (i, r0, rows) in gtiles:
                    k = i - i0
                    parts = []
                    for half in range(nhalf):
                        ap, c0, pkey = halfmap(k, half)
                        parts.append((ap, c0, pkey, wdt * half, wdt))
                    x_out(i, r0, rows, parts, e2c[:, k:k + 1], ("e2c", k))
            pend_mm2[0] = mk_mm2
        while pend_q2[0]:
            pend_q2[0].pop(0)()
        if pend_mm2[0] is not None:
            pend_mm2[0]()

    def p1_ffn1(self):
        cfg, sc = self.cfg, self.sc
        D = cfg.D
        xr = [self.psb(f"x0r{i}", [128, D], F32) for i in range(9)]
        x1st = [self.psb(f"x1st{i}", [128, D], F32) for i in range(2)]
        loaded = {}

        def x_src(i, r0, rows):
            if i not in loaded:
                sc.dma(f"dX{i % 9}", xr[i % 9][:rows, :], self.x0d[r0:r0 + rows, :],
                       writes=[("x0r", i % 9)])
                loaded[i] = True
            return xr[i % 9][:rows, :], ("x0r", i % 9), ()

        def x_out(i, r0, rows, parts, esc_ap, esc_key):
            par = i % 2
            xap, xkey, _ = x_src(i, r0, rows)
            for (ap, c0, pkey, d0, w) in parts:
                sc.op("V", lambda e, rows=rows, ap=ap, c0=c0, d0=d0, w=w,
                      esc_ap=esc_ap, xap=xap: e.scalar_tensor_tensor(
                        x1st[par][:rows, d0:d0 + w], ap[:rows, c0:c0 + w],
                        esc_ap[:rows, :], xap[:rows, d0:d0 + w],
                        AL.mult, AL.add),
                    reads=[pkey, esc_key, xkey], writes=[("x1st", par)])
            sc.dma(f"dX1{par}", self.x1s[r0:r0 + rows, :], x1st[par][:rows, :],
                   reads=[("x1st", par)], writes=[("x1s", i)])

        self.ffn("f1", cfg.R1, cfg.G1, x_src, x_out)

    def p2_proj(self):
        cfg, sc = self.cfg, self.sc
        D, ND, R1, R2 = cfg.D, cfg.ND, cfg.R1, cfg.R2
        tiles = cfg.tiles(R1)

        wq = self.psb("wqsb", [128, ND * D], BF16)
        wk = self.psb("wksb", [128, ND * D], BF16)
        wv = self.psb("wvsb", [128, ND * D], BF16)
        wtmp = [self.psb(f"wld{i}", [128, D], F32) for i in range(2)]
        for m, (wd, wsb, wn) in enumerate([(self.wqd, wq, "wq"), (self.wkd, wk, "wk"),
                                           (self.wvd, wv, "wv")]):
            for fc in range(ND):
                par = (m * ND + fc) % 2
                sc.dma(f"dWp{par}", wtmp[par][:, :], wd[128 * fc:128 * (fc + 1), :],
                       writes=[("wtmp", par)])
                sc.op("V", lambda e, wsb=wsb, fc=fc, par=par: e.tensor_copy(
                    wsb[:, D * fc:D * (fc + 1)], wtmp[par][:, :]),
                    reads=[("wtmp", par)], writes=[wn])

        pb = {k: self.psb("p2" + k, [128, n], F32) for k, n in
              [("bn6", 12), ("mv", 2), ("sq", 1), ("r0", 1), ("r2", 1)]}
        u = [self.psb(f"p2u{i}", [128, D], BF16) for i in range(2)]
        ut = [self.psb(f"p2ut{i}", [128, ND * 512], BF16) for i in range(2)]
        kst = [self.psb(f"p2kst{i}", [128, 512], BF16) for i in range(2)]
        vst = [self.psb(f"p2vst{i}", [128, D], BF16) for i in range(2)]
        xr = [self.psb(f"p2x{i}", [128, D], F32) for i in range(2)]

        pgroups = []
        i = 0
        while i < len(tiles):
            pgroups.append(tiles[i:i + 4])
            i += 4

        def prep(gi, gt):
            par = gi % 2
            gcols = sum(r for (_, _, r) in gt)
            for (j, (i, r0, rows)) in enumerate(gt):
                up = i % 2
                sc.dma(f"dP2x{i % 2}", xr[i % 2][:rows, :],
                       self.x1s[r0:r0 + rows, :],
                       reads=[("x1s", i)], writes=[("p2x", i % 2)])
                if rows < 128:
                    sc.op("V", lambda e, up=up: e.memset(u[up][:, :], 0.0),
                          writes=[("p2u", up)])
                self.ln_tile(pb, ("p2x", i % 2), xr[i % 2][:rows, :], rows,
                             u[up][:rows, :], ("p2u", up), False)
                for fc in range(ND):
                    sc.dma(f"dTq{fc}",
                           ut[par][:, 512 * fc + 128 * j:512 * fc + 128 * j + 128],
                           u[up][:, 128 * fc:128 * (fc + 1)],
                           reads=[("p2u", up)],
                           writes=[("ut", par, fc), ("tqo", fc)],
                           eng="S", transpose=True)
            if gcols < 512:
                for fc in range(ND):
                    sc.op("V", lambda e, fc=fc, gcols=gcols, par=par: e.memset(
                        ut[par][:, 512 * fc + gcols:512 * (fc + 1)], 0.0),
                        writes=[("ut", par, fc)])

        def mms(gi, gt):
            par = gi % 2
            gcols = sum(r for (_, _, r) in gt)
            r0g = gt[0][1]
            qcols = max(0, min(R2 - r0g, gcols))
            for db in range(ND):
                for wsb, wn, isq in [(wq, "wq", True), (wk, "wk", False)]:
                    ncols = qcols if isq else gcols
                    if ncols == 0:
                        continue
                    pps = self.ps1 if isq else self.psC
                    ppk = "ps1" if isq else "psC"
                    for fc in range(ND):
                        sc.op("T", lambda e, db=db, fc=fc, ncols=ncols, wsb=wsb,
                              par=par, pps=pps: e.matmul(
                            pps[:, 0:ncols],
                            wsb[:, D * fc + 128 * db:D * fc + 128 * (db + 1)],
                            ut[par][:, 512 * fc:512 * fc + ncols],
                            start=(fc == 0), stop=(fc == ND - 1)),
                            reads=[wn, ("ut", par, fc)], writes=[ppk])
                    if isq:
                        sc.op("V", lambda e, db=db, r0g=r0g, ncols=ncols:
                            e.tensor_copy(
                                self.qt[:, R2 * db + r0g:R2 * db + r0g + ncols],
                                self.ps1[:, 0:ncols]),
                            reads=["ps1"], writes=[("qt", db)])
                    else:
                        kp_ = db % 2
                        sc.op("V", lambda e, ncols=ncols, kp_=kp_: e.tensor_copy(
                            kst[kp_][:, 0:ncols], self.psC[:, 0:ncols]),
                            reads=["psC"], writes=[("kst", kp_)])
                        sc.dma(f"dK{kp_}",
                               self.kscr.rearrange("(db p) r -> p db r", p=128)
                               [:, db, r0g:r0g + ncols],
                               kst[kp_][:, 0:ncols],
                               reads=[("kst", kp_)], writes=[("kscr", db)])
            for (j, (i, r0, rows)) in enumerate(gt):
                vp_ = j % 2
                mmps = self.psA if vp_ == 0 else self.psB
                mkey = "psA" if vp_ == 0 else "psB"
                nhalf = max(1, D // 512)
                wdt = min(512, D)
                for fc in range(ND):
                    for half in range(nhalf):
                        sc.op("T", lambda e, fc=fc, half=half, j=j, mmps=mmps,
                              wdt=wdt, par=par: e.matmul(
                            mmps[:, wdt * half:wdt * (half + 1)],
                            ut[par][:, 512 * fc + 128 * j:512 * fc + 128 * j + 128],
                            wv[:, D * fc + wdt * half:D * fc + wdt * (half + 1)],
                            start=(fc == 0), stop=(fc == ND - 1)),
                            reads=[("ut", par, fc), "wv"], writes=[mkey])
                sc.op("V", lambda e, rows=rows, mmps=mmps, vp_=vp_: e.tensor_copy(
                    vst[vp_][:rows, :], mmps[:rows, 0:D]),
                    reads=[mkey], writes=[("vst", vp_)])
                sc.dma(f"dV{vp_}", self.vscr[r0:r0 + rows, :], vst[vp_][:rows, :],
                       reads=[("vst", vp_)], writes=[("vscr", i)])

        prep(0, pgroups[0])
        for gi, gt in enumerate(pgroups):
            if gi + 1 < len(pgroups):
                prep(gi + 1, pgroups[gi + 1])
            mms(gi, gt)

    def p3_attn(self):
        cfg, sc = self.cfg, self.sc
        D, H, DK, R1, R2 = cfg.D, cfg.H, cfg.DK, cfg.R1, cfg.R2
        NKT, NP, BW = cfg.NKT, cfg.NP, cfg.BW
        S = cfg.S

        band = self.psb("bandsb", [128, H * BW], F32)
        for h in range(H):
            sc.dma("dBd", band[:, BW * h:BW * (h + 1)], self.bandd[h, :, :],
                   writes=[("band", h)])
        kp = [[self.psb(f"kp{h}_{i}", [128, S], BF16) for i in range(2)]
              for h in range(2)]
        for h in range(2):
            for i in range(2):
                sc.op("V", lambda e, h=h, i=i: e.memset(kp[h][i][:, :], 0.0),
                      writes=[("kp", h, i)])
        vp = [self.psb(f"vp{i}", [128, NKT * 130], BF16) for i in range(2)]
        et = [self.psb(f"et{i}", [128, 1024], BF16) for i in range(3)]
        aot = self.psb("aotp", [128, R2], BF16)
        rrow = self.psb("rrow", [1, 1024], F32)
        dbc = self.psb("dbc", [128, 1024], F32)

        for i in range(2):
            sc.op("V", lambda e, i=i: e.memset(
                vp[i].ap().rearrange("p (kt c) -> p kt c", c=130)[:, :, 64:65], 1.0),
                writes=[("vp", i)])
            sc.op("V", lambda e, i=i: e.memset(
                vp[i].ap().rearrange("p (kt c) -> p kt c", c=130)[:, :, 129:130], 1.0),
                writes=[("vp", i)])

        def load_pair(p):
            par = p % 2
            sc.dma(f"dKp{par}", kp[0][par][0:64, :],
                   self.kscr[128 * p:128 * p + 64, 2:2 + S],
                   writes=[("kp", 0, par)])
            sc.dma(f"dKpb{par}", kp[1][par].ap()[64:128, :],
                   self.kscr[128 * p + 64:128 * (p + 1), 2:2 + S],
                   writes=[("kp", 1, par)])
            sc.dma(f"dVp{par}",
                   vp[par].ap().rearrange("p (kt c) -> p kt c", c=130)[:, :, 0:64],
                   self.vscr[2:2 + S, 128 * p:128 * p + 64].rearrange(
                       "(kt p) d -> p kt d", p=128),
                   writes=[("vp", par)])
            sc.dma(f"dVp{par}",
                   vp[par].ap().rearrange("p (kt c) -> p kt c", c=130)[:, :, 65:129],
                   self.vscr[2:2 + S, 128 * p + 64:128 * (p + 1)].rearrange(
                       "(kt p) d -> p kt d", p=128),
                   writes=[("vp", par)])

        load_pair(0)
        for p in range(NP):
            par = p % 2
            if p + 1 < NP:
                load_pair(p + 1)
            for hh in range(2):
                h = 2 * p + hh
                for (q0, qw) in cfg.QG:
                    def sparts(kt):
                        s = kt % 3
                        if s == 0:
                            return [(self.psA, "psA", 0, 1024, 0)]
                        if s == 1:
                            return [(self.psB, "psB", 0, 1024, 0)]
                        return [(self.ps1, "ps1", 0, 512, 0),
                                (self.psX, "psX", 512, 1024, 512)]

                    def score_mm(kt):
                        for (ap, key, lo, hi, off) in sparts(kt):
                            if lo >= qw:
                                break
                            aa = lo
                            bb = min(qw, hi)
                            while aa < bb:
                                b2 = min(bb, aa + 512)
                                sc.op("T", lambda e, kt=kt, aa=aa, b2=b2, hh=hh,
                                      q0=q0, ap=ap, off=off, par=par, p=p:
                                    e.matmul(
                                        ap[:, aa - off:b2 - off],
                                        kp[hh][par][:, 128 * kt:128 * (kt + 1)],
                                        self.qt.ap().rearrange(
                                            "q (db r) -> q db r", db=cfg.ND)
                                        [:, p, q0 + aa:q0 + b2],
                                        start=True, stop=True),
                                    reads=[("kp", hh, par), ("qt", p)],
                                    writes=[(key, "w")])
                                aa = b2

                    score_mm(0)
                    self._band_prep(h, 0, q0, qw, sparts(0), band)
                    if NKT > 1:
                        score_mm(1)
                        self._band_prep(h, 1, q0, qw, sparts(1), band)
                    for kt in range(NKT):
                        self._exp_regions(h, kt, q0, qw, sparts(kt),
                                          et[kt % 3], ("et", kt % 3), band)
                        if kt + 2 < NKT:
                            score_mm(kt + 2)
                            self._band_prep(h, kt + 2, q0, qw,
                                            sparts(kt + 2), band)
                        aa = 0
                        while aa < qw:
                            bb = min(qw, aa + 512)
                            sc.op("T", lambda e, kt=kt, aa=aa, bb=bb, hh=hh,
                                  par=par: e.matmul(
                                self.psC[0:65, aa:bb],
                                vp[par].ap().rearrange("p (kt c) -> p kt c",
                                                       c=130)
                                [:, kt, 65 * hh:65 * hh + 65],
                                et[kt % 3][:, aa:bb],
                                start=(kt == 0), stop=(kt == NKT - 1)),
                                reads=[("et", kt % 3), ("vp", par)],
                                writes=["psC"])
                            aa = bb
                    sc.op("V", lambda e, qw=qw: e.reciprocal(
                        rrow[0:1, 0:qw], self.psC[64:65, 0:qw]),
                        reads=["psC"], writes=["rrow"])
                    aa = 0
                    while aa < qw:
                        bb = min(qw, aa + 512)
                        sc.op("T", lambda e, aa=aa, bb=bb: e.matmul(
                            self.psA[0:64, aa:bb], self.onessb[0:1, 0:64],
                            rrow[0:1, aa:bb], start=True, stop=True),
                            reads=["rrow", "ones"],
                            writes=[("psA", "w")])
                        sc.op("V", lambda e, aa=aa, bb=bb: e.tensor_copy(
                            dbc[0:64, aa:bb], self.psA[0:64, aa:bb]),
                            reads=[("psA", "w")], writes=["dbc"])
                        aa = bb
                    sc.op("V", lambda e, qw=qw, q0=q0, hh=hh: e.tensor_tensor(
                        aot[64 * hh:64 * hh + 64, q0:q0 + qw],
                        self.psC[0:64, 0:qw], dbc[0:64, 0:qw], AL.mult),
                        reads=["psC", "dbc"], writes=[("aot", hh)])
            sc.dma("dAo", self.aots[128 * p:128 * (p + 1), :], aot[:, :],
                   reads=[("aot", 0), ("aot", 1)], writes=[("aots", p)])

    def _geom(self, kt, q0, qw):
        cfg = self.cfg
        rA = 128 * kt + 3 - cfg.BMAX
        rB = 128 * kt + 129 + cfg.BMAX
        cA = max(0, min(qw, rA - q0))
        cB = max(0, min(qw, rB - q0))
        return rA, cA, cB

    def _band_prep(self, h, kt, q0, qw, parts, band):
        cfg, sc = self.cfg, self.sc
        BW = cfg.BW
        rA, cA, cB = self._geom(kt, q0, qw)
        for (ap, key, lo, hi, off) in parts:
            a = max(max(lo, 0), cA)
            b = min(min(hi, qw), cB)
            if b > a:
                j0 = (q0 + a) - rA
                sc.op("V", lambda e, a=a, b=b, j0=j0, ap=ap, off=off:
                    e.tensor_tensor(
                        ap[:, a - off:b - off], ap[:, a - off:b - off],
                        band[:, BW * h + j0:BW * h + j0 + (b - a)], AL.add),
                    reads=[(key, "w"), ("band", h)], writes=[(key, "w")])

    def _exp_regions(self, h, kt, q0, qw, parts, etbuf, etkey, band):
        cfg, sc = self.cfg, self.sc
        rA, cA, cB = self._geom(kt, q0, qw)
        ca_ap = self.ccsb[:, h:h + 1]
        cb_ap = self.ccsb[:, cfg.H + h:cfg.H + h + 1]
        for (ap, key, lo, hi, off) in parts:
            a0, b0 = max(lo, 0), min(hi, qw)
            if b0 <= a0:
                continue
            a, b = a0, min(b0, cA)
            if b > a:
                sc.op("S", lambda e, a=a, b=b, ap=ap, off=off: e.activation(
                    etbuf[:, a:b], ap[:, a - off:b - off], AF.Exp, bias=ca_ap),
                    reads=[(key, "w"), "cc"], writes=[etkey])
            a, b = max(a0, cA), min(b0, cB)
            if b > a:
                sc.op("S", lambda e, a=a, b=b, ap=ap, off=off: e.activation(
                    etbuf[:, a:b], ap[:, a - off:b - off], AF.Exp),
                    reads=[(key, "w")], writes=[etkey])
            a, b = max(a0, cB), b0
            if b > a:
                sc.op("S", lambda e, a=a, b=b, ap=ap, off=off: e.activation(
                    etbuf[:, a:b], ap[:, a - off:b - off], AF.Exp, bias=cb_ap),
                    reads=[(key, "w"), "cc"], writes=[etkey])

    def p4_wo(self):
        cfg, sc = self.cfg, self.sc
        D, ND, R2 = cfg.D, cfg.ND, cfg.R2
        tiles = cfg.tiles(R2)

        wo = self.psb("wosb", [128, ND * D], BF16)
        wtmp = [self.psb(f"p4w{i}", [128, D], F32) for i in range(2)]
        for fc in range(ND):
            par = fc % 2
            sc.dma(f"dWp{par}", wtmp[par][:, :], self.wod[128 * fc:128 * (fc + 1), :],
                   writes=[("wtmp", par)])
            sc.op("V", lambda e, fc=fc, par=par: e.tensor_copy(
                wo[:, D * fc:D * (fc + 1)], wtmp[par][:, :]),
                reads=[("wtmp", par)], writes=["wo"])

        aotr = [self.psb(f"p4a{i}", [128, ND * 128], BF16) for i in range(2)]
        x1r = [self.psb(f"p4x{i}", [128, D], F32) for i in range(2)]
        x2st = [self.psb(f"p4o{i}", [128, D], F32) for i in range(2)]
        for i_ in range(2):
            sc.op("V", lambda e, i_=i_: e.memset(aotr[i_][:, :], 0.0),
                  writes=[("p4a", i_)])
        for (i, r0, rows) in tiles:
            par = i % 2
            sc.dma(f"dA{par}",
                   aotr[par].ap().rearrange("p (f c) -> p f c", f=ND)[:, :, 0:rows],
                   self.aots.rearrange("(f p) r -> p f r", p=128)[:, :, r0:r0 + rows],
                   writes=[("p4a", par)])
            sc.dma(f"dP4x{par}", x1r[par][:rows, :], self.x1s[r0:r0 + rows, :],
                   writes=[("p4x", par)])
            mmps = self.psA if par == 0 else self.psB
            mkey = "psA" if par == 0 else "psB"
            nhalf = max(1, D // 512)
            wdt = min(512, D)
            for fc in range(ND):
                for half in range(nhalf):
                    sc.op("T", lambda e, fc=fc, half=half, par=par, mmps=mmps,
                          wdt=wdt: e.matmul(
                        mmps[:, wdt * half:wdt * (half + 1)],
                        aotr[par][:, 128 * fc:128 * (fc + 1)],
                        wo[:, D * fc + wdt * half:D * fc + wdt * (half + 1)],
                        start=(fc == 0), stop=(fc == ND - 1)),
                        reads=[("p4a", par), "wo"], writes=[mkey])
            sc.op("V", lambda e, i=i, rows=rows, mmps=mmps, par=par:
                e.tensor_tensor(
                    x2st[par][:rows, :], mmps[:rows, 0:D],
                    x1r[par][:rows, :], AL.add),
                reads=[mkey, ("p4x", par)], writes=[("x2st", par)])
            if i == 0:
                sc.op("V", lambda e: e.memset(x2st[0][0:2, 0:D], 0.0),
                      writes=[("x2st", 0)])
            sc.dma(f"dX2o{par}", self.x2s[r0:r0 + rows, :],
                   x2st[par][:rows, :],
                   reads=[("x2st", par)], writes=[("x2s", i)])

    def p5_ffn2(self):
        cfg, sc = self.cfg, self.sc
        D = cfg.D
        xr = [self.psb(f"x2r{i}", [128, D], F32) for i in range(9)]
        outst = [self.psb(f"outst{i}", [128, D], F32) for i in range(2)]
        loaded = {}

        def x_src(i, r0, rows):
            if i not in loaded:
                sc.dma(f"dX2{i % 9}", xr[i % 9][:rows, :],
                       self.x2s[r0:r0 + rows, :],
                       reads=[("x2s", i)], writes=[("x2r", i % 9)])
                loaded[i] = True
            return xr[i % 9][:rows, :], ("x2r", i % 9), ()

        def x_out(i, r0, rows, parts, esc_ap, esc_key):
            par = i % 2
            xap, xkey, _ = x_src(i, r0, rows)
            for (ap, c0, pkey, d0, w) in parts:
                sc.op("V", lambda e, rows=rows, ap=ap, c0=c0, d0=d0, w=w,
                      esc_ap=esc_ap, xap=xap: e.scalar_tensor_tensor(
                        outst[par][:rows, d0:d0 + w], ap[:rows, c0:c0 + w],
                        esc_ap[:rows, :], xap[:rows, d0:d0 + w],
                        AL.mult, AL.add),
                    reads=[pkey, esc_key, xkey], writes=[("outst", par)])
            sc.dma(f"dOut{par}", self.outd[r0:r0 + rows, :],
                   outst[par][:rows, :],
                   reads=[("outst", par)], writes=[("out", i)])

        self.ffn("f2", cfg.R2, cfg.G2, x_src, x_out)

    def p6_out(self):
        cfg, sc = self.cfg, self.sc
        sc.op("G", lambda e: e.memset(self.stmp[:, 7:8], 0.0),
              reads=[("out", i) for (i, _, _) in cfg.tiles(cfg.R2)])


def _stage_core(cfg, core, x, w1, convw, convb, w2, wq, wk, wv, wo, table):
    b, par = core // 2, core % 2
    S, D = cfg.S, cfg.D
    xb = np.ascontiguousarray(np.asarray(x[b], dtype=np.float32))
    if par == 1:
        xb = np.ascontiguousarray(xb[::-1])
    x0 = np.zeros((cfg.R1, D), dtype=np.float32)
    x0[2:2 + S] = xb

    cw = np.asarray(convw, dtype=np.float32).reshape(cfg.C, 3)
    if par == 1:
        cw = cw[:, ::-1]
    cwl = np.zeros((128, cfg.NC, 3), dtype=np.float32)
    cbl = np.zeros((128, cfg.NC), dtype=np.float32)
    cb_ = np.asarray(convb, dtype=np.float32)
    for ct in range(cfg.NC):
        cwl[:, ct, :] = cw[128 * ct:128 * (ct + 1), :]
        cbl[:, ct] = cb_[128 * ct:128 * (ct + 1)]

    tbl = np.asarray(table, dtype=np.float32)
    p_ = np.arange(128)[:, None]
    j_ = np.arange(cfg.BW)[None, :]
    d_loc = p_ - j_ + (cfg.BMAX - 1)
    d_glob = d_loc if par == 0 else -d_loc
    bucket = _bucket_np(d_glob, cfg.NBUCK, cfg.MAXD)
    band = np.ascontiguousarray(
        np.transpose(tbl[bucket], (2, 0, 1)).astype(np.float32))

    bA = int(_bucket_np(np.array(cfg.BMAX if par == 0 else -cfg.BMAX),
                        cfg.NBUCK, cfg.MAXD))
    bB = int(_bucket_np(np.array(-cfg.BMAX if par == 0 else cfg.BMAX),
                        cfg.NBUCK, cfg.MAXD))
    cc = np.zeros((128, 2 * cfg.H), dtype=np.float32)
    cc[:, 0:cfg.H] = tbl[bA][None, :]
    cc[:, cfg.H:] = tbl[bB][None, :]

    return {
        "x0": x0,
        "w1": np.ascontiguousarray(np.asarray(w1, dtype=np.float32)),
        "w2": np.ascontiguousarray(np.asarray(w2, dtype=np.float32)),
        "cw": cwl.reshape(128, cfg.NC * 3), "cb": cbl,
        "wq": np.ascontiguousarray(np.asarray(wq, dtype=np.float32)),
        "wk": np.ascontiguousarray(np.asarray(wk, dtype=np.float32)),
        "wv": np.ascontiguousarray(np.asarray(wv, dtype=np.float32)),
        "wo": np.ascontiguousarray(np.asarray(wo, dtype=np.float32)),
        "band": band, "cc": cc,
        "ident": np.eye(128, dtype=np.float32),
        "identb": np.eye(128).astype(ml_dtypes.bfloat16),
        "ones": np.ones((128, 128), dtype=np.float32),
    }


_NC_CACHE = {}


def _get_nc(cfg):
    key = (cfg.D, cfg.C, cfg.H, cfg.S, cfg.G1, cfg.G2)
    if key not in _NC_CACHE:
        _NC_CACHE[key] = Build(cfg).build()
    return _NC_CACHE[key]


def kernel(L0_attn_hidden_states, L0_attn_Wq, L0_attn_Wk, L0_attn_Wv, L0_attn_Wo,
           L0_attn_rel_bias_table, L0_ffn_w1, L0_ffn_conv_w, L0_ffn_conv_b,
           L0_ffn_w2):
    cfg = Cfg()
    x = np.asarray(L0_attn_hidden_states)
    in_maps = [_stage_core(cfg, c, x, L0_ffn_w1, L0_ffn_conv_w, L0_ffn_conv_b,
                           L0_ffn_w2, L0_attn_Wq, L0_attn_Wk, L0_attn_Wv,
                           L0_attn_Wo, L0_attn_rel_bias_table)
               for c in range(8)]
    nc = _get_nc(cfg)
    res = run_bass_kernel_spmd(nc, in_maps, core_ids=list(range(8)))
    out = np.zeros((cfg.B, cfg.S, cfg.D), dtype=np.float32)
    half = cfg.S // 2
    for c in range(8):
        b, par = c // 2, c % 2
        o = np.asarray(res.results[c]["out"])[2:2 + half]
        if par == 0:
            out[b, :half] = o
        else:
            out[b, half:] = o[::-1]
    return out
